# version 05
# baseline (speedup 1.0000x reference)
import os
import numpy as np
import ml_dtypes
from contextlib import ExitStack

import concourse.bass as bass
import concourse.mybir as mybir
from concourse.bass_utils import run_bass_kernel_spmd

F32 = mybir.dt.float32
BF16 = mybir.dt.bfloat16
AF = mybir.ActivationFunctionType
ALU = mybir.AluOpType
AX = mybir.AxisListType
bf16 = ml_dtypes.bfloat16

NCORE = 8
T = 2048
D = 2048
NT = 16
NDC = 16
H = 16
HD = 128
M = 4
NSC = 8
EPS_ATT = 1e-6
EPS_LN = 1e-5
RELSC = (HD ** -0.5) / 16384.0

DEBUG = bool(int(os.environ.get("MK_DEBUG", "0")))

ENGS = ("sync", "gpsimd", "tensor", "vector", "scalar")
ESEM = {"sync": "sy", "gpsimd": "gd", "tensor": "pe", "vector": "dv", "scalar": "ac"}


class Res:
    __slots__ = ("writes", "reads", "excl", "name")

    def __init__(self, name="", excl=False):
        self.writes = {}
        self.reads = {}
        self.excl = excl
        self.name = name


class Prog:
    def __init__(self, sems):
        self.sems = sems
        self.ops = {e: [] for e in ENGS}
        self.cnt = {s: 0 for s in sems}
        self.lastwait = {e: {} for e in ENGS}

    def emit(self, eng, fn, waits=(), inc=None, incsem=None):
        sem = incsem or ESEM[eng]
        ws = []
        for s, v in waits:
            if v <= 0:
                continue
            assert v <= self.cnt[s], (
                f"wait-on-future: {eng} waits {s}>={v}, emitted {self.cnt[s]}")
            if s == ESEM[eng] and eng == "tensor":
                continue
            if self.lastwait[eng].get(s, 0) >= v:
                continue
            self.lastwait[eng][s] = v
            ws.append((s, v))
        mil = None
        if inc:
            self.cnt[sem] += inc
            mil = (sem, self.cnt[sem])
        self.ops[eng].append((tuple(ws), fn, sem if inc else None, inc))
        return mil

    def rw(self, res, eng):
        w = list(res.writes.items())
        if res.excl:
            for s, v in res.reads.items():
                if s != ESEM[eng]:
                    w.append((s, v))
        return w

    def ww(self, res, eng):
        return list(res.writes.items()) + list(res.reads.items())

    @staticmethod
    def note_r(res, mil):
        if mil is not None:
            s, v = mil
            if res.reads.get(s, 0) < v:
                res.reads[s] = v

    @staticmethod
    def note_w(res, mil):
        if mil is not None:
            res.writes = {mil[0]: mil[1]}
            res.reads = {}


class Buf:
    __slots__ = ("t", "r")

    def __init__(self, t, name, excl=False):
        self.t = t
        self.r = Res(name, excl)


def build_program():
    nc = bass.Bass("TRN2", target_bir_lowering=False, debug=False)

    def din(name, shape):
        return nc.dram_tensor(name, list(shape), F32, kind="ExternalInput").ap()

    x_d = din("x", (T, D))
    pmask_d = din("pmask", (1, NCORE))
    wqT_d = din("wqT", (D, D))
    wkT_d = din("wkT", (D, D))
    wvT_d = din("wvT", (D, D))
    woT_d = din("woT", (D, D))
    memr_d = din("memr", (H, M, HD, HD))
    mnT_d = din("mnT", (HD, H * M))
    mnF_d = din("mnF", (M, H * HD))
    gate_d = din("gatev", (1, H))
    G_d = din("G", (D, 8))
    laT_d = din("laT", (D, 128))
    gupg_d = din("gupTg", (D, D))
    gupu_d = din("gupTu", (D, D))
    downT_d = din("downT", (D, D))
    lbT_d = din("lbT", (128, D))

    y_d = nc.dram_tensor("y", [T, D], F32, kind="ExternalOutput").ap()

    skind = "ExternalOutput" if DEBUG else "Internal"
    sq_scr = nc.dram_tensor("sq_scr", [H, HD, T], BF16, kind=skind).ap()
    at_scr = nc.dram_tensor("at_scr", [H, HD, T], BF16, kind=skind).ap()
    xp_scr = nc.dram_tensor("xp_scr", [NT, 128, D], F32, kind=skind).ap()
    if DEBUG:
        x1dbg = nc.dram_tensor("x1dbg", [128, NDC * T], BF16, kind="ExternalOutput").ap()
        qdbg = nc.dram_tensor("qdbg", [128, T], F32, kind="ExternalOutput").ap()
        x2dbg = nc.dram_tensor("x2dbg", [128, NDC * T], BF16, kind="ExternalOutput").ap()
        gadbg = nc.dram_tensor("gadbg", [128, NDC * T], BF16, kind="ExternalOutput").ap()
        smdbg = nc.dram_tensor("smdbg", [128, 64], F32, kind="ExternalOutput").ap()
        zbdbg = nc.dram_tensor("zbdbg", [128, 16], F32, kind="ExternalOutput").ap()
        rtdbg = nc.dram_tensor("rtdbg", [128, 40], F32, kind="ExternalOutput").ap()
        wldbg = nc.dram_tensor("wldbg", [128, 128], F32, kind="ExternalOutput").ap()
    cc_in = nc.dram_tensor("cc_in", [128, 16], F32, kind="Internal")
    cc_out = nc.dram_tensor("cc_out", [128, 16], F32, kind="Internal")
    den_scr = nc.dram_tensor("den_scr", [H, 1, T], F32, kind=skind).ap()
    st_scr = nc.dram_tensor("st_scr", [H * HD, 132], BF16, kind="Internal")
    stg_all = nc.dram_tensor("stg_all", [NCORE * H * HD, 132], BF16, kind="Internal")

    identb_h = nc.inline_tensor(np.eye(128, dtype=np.float32).astype(bf16), name="c_identb")
    identf_h = nc.inline_tensor(np.eye(128, dtype=np.float32), name="c_identf")
    triu = np.triu(np.ones((128, 128), np.float32))
    mask3_h = nc.inline_tensor(
        np.concatenate([triu, np.ones((128, 128), np.float32), triu], 1).astype(bf16),
        name="c_mask3")
    onesb_h = nc.inline_tensor(np.ones((128, 512), np.float32).astype(bf16), name="c_onesb")
    onesf_h = nc.inline_tensor(np.ones((128, 16), np.float32), name="c_onesf")
    onesfr_h = nc.inline_tensor(np.ones((1, 128), np.float32), name="c_onesfr")
    epsr_h = nc.inline_tensor(np.full((1, 8), EPS_ATT, np.float32).astype(bf16), name="c_epsr")
    sel_np = np.zeros((8, 8 * 128), np.float32)
    for e in range(8):
        sel_np[e, e * 128:(e + 1) * 128] = 1.0
    selmat_h = nc.inline_tensor(sel_np.astype(bf16), name="c_selmat")


    es = ExitStack()
    ec = es.enter_context

    def S(name, shape, dt):
        return Buf(ec(nc.sbuf_tensor(name, list(shape), dt)), name)

    arenaA = S("arenaA", (128, NDC * T), BF16)
    arenaB = S("arenaB", (128, NDC * T), BF16)
    fp = [S(f"fp{i}", (128, D), F32) for i in range(3)]
    bp = [S(f"bp{i}", (128, D), BF16) for i in range(6)]
    wsl = [S(f"wsl{i}", (128, D), BF16) for i in range(4)]
    msl = [S(f"msl{i}", (128, 512), BF16) for i in range(2)]
    gaG = S("gaG", (128, NDC * 8), BF16)
    lrw = S("lrw", (128, T), BF16)
    scb = [S(f"scb{i}", (128, 384), BF16) for i in range(2)]
    stsb = [S(f"stsb{i}", (128, 132), BF16) for i in range(2)]
    identb = S("identb", (128, 128), BF16)
    identf = S("identf", (128, 128), F32)
    mask3 = S("mask3", (128, 384), BF16)
    onesb = S("onesb", (128, 512), BF16)
    onesf = S("onesf", (128, 16), F32)
    onesfr = S("onesfr", (1, 128), F32)
    epsr = S("epsr", (1, 8), BF16)
    selmat = S("selmat", (8, 8 * 128), BF16)
    mnTf = S("mnTf", (128, H * M), F32)
    mnTb = S("mnTb", (128, H * M), BF16)
    qsum = S("qsum", (128, H), F32)
    gavs = S("gavs", (1, H), F32)
    g1bc = S("g1bc", (128, H), F32)
    gTc = S("gTc", (16, 1), F32)
    sm4 = S("sm4", (4, 8), F32)
    nta = S("nta", (1, 2), F32)
    nta16 = S("nta16", (16, 1), F32)
    relT = S("relT", (16, 8), F32)
    m1c = S("m1c", (16, 4), F32)
    wgT = S("wgT", (16, 4), F32)
    wgmh = S("wgmh", (4, 16), F32)
    zbuf = S("zbuf", (128, 16), F32)
    relg = zbuf
    lntmp = S("lntmp", (128, 48), F32)
    pmk = S("pmk", (1, NCORE), F32)
    pmkbc = S("pmkbc", (128, NCORE), F32)
    s1a = S("s1a", (128, NT), F32)
    ssqa = S("ssqa", (128, NT), F32)
    rstda = S("rstda", (128, NT), F32)
    ngmra = S("ngmra", (128, NT), F32)
    wall = S("wall", (128, NT * 8), F32)
    wbc = bp[2]
    wT8 = bp[5]
    rdrow = lrw
    wexp = S("wexp", (128, 128), BF16)
    lg = S("lg", (128, 32), F32)
    r1 = S("r1", (128, 8), F32)

    ps_a = ec(nc.psum_tensor("ps_a", [128, 2048], F32))
    ps_b = ec(nc.psum_tensor("ps_b", [128, 2048], F32))
    bankres = {("a", i): Res(f"pa{i}", excl=True) for i in range(4)}
    bankres.update({("b", i): Res(f"pb{i}", excl=True) for i in range(4)})

    def banks(t, c0, c1):
        nm = "a" if t is ps_a else "b"
        return [bankres[(nm, i)] for i in range(c0 // 512, (c1 + 511) // 512)]

    res_sq = Res("sq_scr")
    res_at = Res("at_scr")
    res_xp = Res("xp_scr")
    res_ccin = Res("cc_in")
    res_ccout = Res("cc_out")
    res_den = Res("den_scr")
    res_st = Res("st_scr")
    res_stg = Res("stg_all")

    sems = {s: ec(nc.semaphore(s)) for s in ("sy", "gd", "pe", "dv", "ac", "cc")}
    p = Prog(sems)

    def dma(eng, dst, src, w=(), r=(), xw=()):
        waits = list(xw)
        for res in w:
            waits += p.ww(res, eng)
        for res in r:
            waits += p.rw(res, eng)
        mil = p.emit(eng, (lambda e, d=dst, s=src: e.dma_start(out=d, in_=s)),
                     waits=waits, inc=16)
        for res in w:
            Prog.note_w(res, mil)
        for res in r:
            Prog.note_r(res, mil)
        return mil

    def comp(eng, fn, w=(), r=(), xw=(), inc=1, incsem=None):
        waits = list(xw)
        for res in w:
            waits += p.ww(res, eng)
        for res in r:
            waits += p.rw(res, eng)
        mil = p.emit(eng, fn, waits=waits, inc=inc, incsem=incsem)
        for res in w:
            Prog.note_w(res, mil)
        for res in r:
            Prog.note_r(res, mil)
        return mil

    def mm(outbanks, steps, r=()):
        waits = []
        for res in outbanks:
            waits += p.ww(res, "tensor")
        for res in r:
            waits += p.rw(res, "tensor")
        mil = None
        n = len(steps)
        for i, fn in enumerate(steps):
            mil = p.emit("tensor", fn, waits=(waits if i == 0 else ()),
                         inc=(1 if i == n - 1 else None))
        for res in outbanks:
            Prog.note_w(res, mil)
        for res in r:
            Prog.note_r(res, mil)
        return mil

    for h_, b_ in ((identb_h, identb), (identf_h, identf), (mask3_h, mask3),
                   (onesb_h, onesb), (onesf_h, onesf), (onesfr_h, onesfr),
                   (epsr_h, epsr), (selmat_h, selmat)):
        dma("sync", b_.t[:], h_.ap(), w=(b_.r,))
    dma("sync", mnTf.t[:], mnT_d, w=(mnTf.r,))
    dma("gpsimd", mnTb.t[:], mnT_d, w=(mnTb.r,))
    dma("sync", fp[0].t[0:M, :], mnF_d, w=(fp[0].r,))
    dma("sync", gavs.t[:], gate_d, w=(gavs.r,))
    dma("sync", pmk.t[:], pmask_d, w=(pmk.r,))
    comp("gpsimd", lambda e: e.memset(zbuf.t[:], 0.0), w=(zbuf.r,))

    comp("scalar", lambda e: e.activation(out=gavs.t[:], in_=gavs.t[:], func=AF.Sigmoid),
         w=(gavs.r,), r=(gavs.r,))
    mm(banks(ps_a, 0, 1),
       [lambda e: e.transpose(ps_a[0:16, 0:1], gavs.t[:], identf.t[0:1, 0:1])],
       r=(gavs.r, identf.r))
    comp("vector", lambda e: e.tensor_copy(out=gTc.t[:], in_=ps_a[0:16, 0:1]),
         w=(gTc.r,), r=(banks(ps_a, 0, 1)[0],))
    mm(banks(ps_a, 512, 528),
       [lambda e: e.matmul(ps_a[:, 512:528], onesfr.t[:], gavs.t[:], start=True, stop=True)],
       r=(onesfr.r, gavs.r))
    comp("vector", lambda e: e.tensor_scalar(out=g1bc.t[:], in0=ps_a[:, 512:528],
                                             scalar1=-1.0, scalar2=1.0,
                                             op0=ALU.mult, op1=ALU.add),
         w=(g1bc.r,), r=(banks(ps_a, 512, 528)[0],))
    comp("vector", lambda e: e.reduce_sum(out=sm4.t[0:4, 0:1], in_=fp[0].t[0:4, :], axis=AX.X),
         w=(sm4.r,), r=(fp[0].r,))
    comp("vector", lambda e: e.tensor_scalar(out=sm4.t[0:4, 1:2], in0=sm4.t[0:4, 0:1],
                                             scalar1=EPS_ATT, scalar2=None, op0=ALU.is_lt),
         w=(sm4.r,), r=(sm4.r,))
    comp("vector", lambda e: e.tensor_scalar(out=sm4.t[0:4, 2:3], in0=sm4.t[0:4, 1:2],
                                             scalar1=-1e30, scalar2=None, op0=ALU.mult),
         w=(sm4.r,), r=(sm4.r,))
    mm(banks(ps_a, 528, 529),
       [lambda e: e.matmul(ps_a[0:1, 528:529], sm4.t[0:4, 1:2], onesf.t[0:4, 0:1],
                           start=True, stop=True)],
       r=(sm4.r, onesf.r))
    comp("vector", lambda e: e.tensor_scalar(out=nta.t[0:1, 0:1], in0=ps_a[0:1, 528:529],
                                             scalar1=3.5, scalar2=None, op0=ALU.is_lt),
         w=(nta.r,), r=(banks(ps_a, 528, 529)[0],))
    mm(banks(ps_a, 529, 530),
       [lambda e: e.matmul(ps_a[0:16, 529:530], onesfr.t[0:1, 0:16], nta.t[0:1, 0:1],
                           start=True, stop=True)],
       r=(onesfr.r, nta.r))
    comp("vector", lambda e: e.tensor_copy(out=nta16.t[:], in_=ps_a[0:16, 529:530]),
         w=(nta16.r,), r=(banks(ps_a, 529, 530)[0],))
    mm(banks(ps_a, 544, 552),
       [lambda e: e.matmul(ps_a[:, 544:552], onesfr.t[:], pmk.t[:], start=True, stop=True)],
       r=(onesfr.r, pmk.r))
    comp("vector", lambda e: e.tensor_copy(out=pmkbc.t[:], in_=ps_a[:, 544:552]),
         w=(pmkbc.r,), r=banks(ps_a, 544, 552))

    for st in range(NT):
        fb = fp[st % 2]
        dma("sync", fb.t[:], x_d[st * 128:(st + 1) * 128, :], w=(fb.r,))
        comp("vector", lambda e, st=st, fb=fb: e.reduce_sum(
            out=s1a.t[:, st:st + 1], in_=fb.t[:], axis=AX.X),
            w=(s1a.r,), r=(fb.r,))
        comp("scalar", lambda e, st=st, fb=fb: e.activation(
            out=fp[2].t[:], in_=fb.t[:], func=AF.Square, accum_out=ssqa.t[:, st:st + 1]),
            w=(fp[2].r, ssqa.r), r=(fb.r,))
    mean_ap = lntmp.t[:, 0:16]
    t1_ap = lntmp.t[:, 16:32]
    comp("vector", lambda e: e.tensor_scalar(out=mean_ap, in0=s1a.t[:], scalar1=1.0 / D,
                                             scalar2=None, op0=ALU.mult),
         w=(lntmp.r,), r=(s1a.r,))
    comp("vector", lambda e: e.tensor_scalar(out=t1_ap, in0=ssqa.t[:], scalar1=1.0 / D,
                                             scalar2=None, op0=ALU.mult),
         w=(lntmp.r,), r=(ssqa.r, lntmp.r))
    comp("vector", lambda e: e.tensor_tensor(out=lntmp.t[:, 32:48], in0=mean_ap,
                                             in1=mean_ap, op=ALU.mult),
         w=(lntmp.r,), r=(lntmp.r,))
    comp("vector", lambda e: e.tensor_tensor(out=t1_ap, in0=t1_ap,
                                             in1=lntmp.t[:, 32:48], op=ALU.subtract),
         w=(lntmp.r,), r=(lntmp.r,))
    comp("vector", lambda e: e.tensor_scalar(out=t1_ap, in0=t1_ap, scalar1=EPS_LN,
                                             scalar2=None, op0=ALU.add),
         w=(lntmp.r,), r=(lntmp.r,))
    comp("scalar", lambda e: e.activation(out=lntmp.t[:, 32:48], in_=t1_ap, func=AF.Sqrt),
         w=(lntmp.r,), r=(lntmp.r,))
    comp("vector", lambda e: e.reciprocal(out=rstda.t[:], in_=lntmp.t[:, 32:48]),
         w=(rstda.r,), r=(lntmp.r,))
    comp("vector", lambda e: e.scalar_tensor_tensor(out=ngmra.t[:], in0=mean_ap,
                                                    scalar=-1.0, in1=rstda.t[:],
                                                    op0=ALU.mult, op1=ALU.mult),
         w=(ngmra.r,), r=(lntmp.r, rstda.r))
    for st in range(NT):
        fb = fp[st % 2]
        xb = bp[st % 2]
        dma("sync", fb.t[:], x_d[st * 128:(st + 1) * 128, :], w=(fb.r,))
        comp("gpsimd", lambda e, st=st, fb=fb, xb=xb: e.tensor_scalar(
            out=xb.t[:], in0=fb.t[:], scalar1=rstda.t[:, st:st + 1],
            scalar2=ngmra.t[:, st:st + 1], op0=ALU.mult, op1=ALU.add),
            w=(xb.r,), r=(fb.r, rstda.r, ngmra.r))
        for half in range(2):
            pv = ps_b[:, half * 512:(half + 1) * 512].bitcast(BF16)
            bres = banks(ps_b, half * 512, (half + 1) * 512)
            steps = []
            for k in range(8):
                dc = half * 8 + k
                steps.append(lambda e, k=k, dc=dc, pv=pv, xb=xb: e.transpose(
                    pv[:, k * 128:(k + 1) * 128], xb.t[:, dc * 128:(dc + 1) * 128],
                    identb.t[:]))
            mm(bres, steps, r=(xb.r, identb.r))
            outap = arenaA.t[:].rearrange("p (dc s) -> p dc s", s=T)[
                :, half * 8:(half + 1) * 8, st * 128:(st + 1) * 128]
            comp("scalar", lambda e, pv=pv, outap=outap: e.activation(
                out=outap, in_=pv.rearrange("p (dc s) -> p dc s", s=128), func=AF.Copy),
                w=(arenaA.r,), r=tuple(bres))

    if DEBUG:
        dma("sync", x1dbg, arenaA.t[:], r=(arenaA.r,))

    a_all = banks(ps_a, 0, 2048)
    b_all = banks(ps_b, 0, 2048)

    def load_wslab(slot, wsrc, h):
        src = wsrc[:, h * 128:(h + 1) * 128].rearrange("(dc p) j -> p dc j", p=128)
        dst = slot.t[:].rearrange("p (dc j) -> p dc j", j=128)
        dma("gpsimd", dst, src, w=(slot.r,))

    sqh, skh, kph, vh, rdb, locb = bp[0], bp[1], bp[2], bp[3], bp[4], bp[5]
    etmp = bp[4]

    for h in range(H):
        slq, slk, slv = wsl[(3 * h) % 4], wsl[(3 * h + 1) % 4], wsl[(3 * h + 2) % 4]
        load_wslab(slq, wqT_d, h)
        load_wslab(slk, wkT_d, h)
        load_wslab(slv, wvT_d, h)

        steps = []
        for dc in range(NDC):
            for nb in range(4):
                steps.append(lambda e, dc=dc, nb=nb, slq=slq: e.matmul(
                    ps_a[:, nb * 512:(nb + 1) * 512], slq.t[:, dc * 128:(dc + 1) * 128],
                    arenaA.t[:, dc * T + nb * 512: dc * T + (nb + 1) * 512],
                    start=(dc == 0), stop=(dc == NDC - 1)))
        mm(a_all, steps, r=(slq.r, arenaA.r))
        if DEBUG and h == 0:
            comp("vector", lambda e: e.tensor_copy(out=fp[1].t[:], in_=ps_a[:, 0:2048]),
                 w=(fp[1].r,), r=a_all)
            dma("sync", qdbg, fp[1].t[:], r=(fp[1].r,))
        comp("vector", lambda e, h=h: e.reduce_sum(out=qsum.t[:, h:h + 1], in_=ps_a[:, 0:2048],
                                                   axis=AX.X),
             w=(qsum.r,), r=a_all)
        comp("scalar", lambda e: e.activation(out=fp[2].t[:], in_=ps_a[:, 0:2048],
                                              func=AF.Relu, scale=-1.0),
             w=(fp[2].r,), r=a_all)
        comp("scalar", lambda e: e.activation(out=etmp.t[:], in_=fp[2].t[:],
                                              func=AF.Exp, scale=-1.0),
             w=(etmp.r,), r=(fp[2].r,))
        comp("vector", lambda e: e.scalar_tensor_tensor(out=sqh.t[:], in0=ps_a[:, 0:2048],
                                                        scalar=0.0, in1=etmp.t[:],
                                                        op0=ALU.max, op1=ALU.add),
             w=(sqh.r,), r=a_all + [etmp.r])
        dma("sync", sq_scr[h], sqh.t[:].rearrange("p s -> p s"), w=(res_sq,), r=(sqh.r,))

        steps = []
        for dc in range(NDC):
            for nb in range(4):
                steps.append(lambda e, dc=dc, nb=nb, slk=slk: e.matmul(
                    ps_b[:, nb * 512:(nb + 1) * 512], slk.t[:, dc * 128:(dc + 1) * 128],
                    arenaA.t[:, dc * T + nb * 512: dc * T + (nb + 1) * 512],
                    start=(dc == 0), stop=(dc == NDC - 1)))
        mm(b_all, steps, r=(slk.r, arenaA.r))
        comp("scalar", lambda e: e.activation(out=fp[2].t[:], in_=ps_b[:, 0:2048],
                                              func=AF.Relu, scale=-1.0),
             w=(fp[2].r,), r=b_all)
        comp("scalar", lambda e: e.activation(out=etmp.t[:], in_=fp[2].t[:],
                                              func=AF.Exp, scale=-1.0),
             w=(etmp.r,), r=(fp[2].r,))
        comp("vector", lambda e: e.scalar_tensor_tensor(out=skh.t[:], in0=ps_b[:, 0:2048],
                                                        scalar=0.0, in1=etmp.t[:],
                                                        op0=ALU.max, op1=ALU.add),
             w=(skh.r,), r=b_all + [etmp.r])

        for g4 in range(4):
            for sti in range(4):
                st = g4 * 4 + sti
                steps = []
                for dc in range(NDC):
                    steps.append(lambda e, dc=dc, st=st, sti=sti, slv=slv: e.matmul(
                        ps_a[:, sti * 512: sti * 512 + 128],
                        arenaA.t[:, dc * T + st * 128: dc * T + (st + 1) * 128],
                        slv.t[:, dc * 128:(dc + 1) * 128],
                        start=(dc == 0), stop=(dc == NDC - 1)))
                mm(banks(ps_a, sti * 512, sti * 512 + 128), steps, r=(slv.r, arenaA.r))
            inap = ps_a[:, 0:2048].rearrange("p (b c) -> p b c", c=512)[:, 0:4, 0:128]
            outap = vh.t[:, g4 * 512:(g4 + 1) * 512].rearrange("p (b c) -> p b c", c=128)
            comp("scalar", lambda e, inap=inap, outap=outap: e.activation(
                out=outap, in_=inap, func=AF.Copy),
                w=(vh.r,), r=a_all)

        for half in range(2):
            pv = ps_b[:, half * 512:(half + 1) * 512].bitcast(BF16)
            bres = banks(ps_b, half * 512, (half + 1) * 512)
            steps = []
            for k in range(8):
                st = half * 8 + k
                steps.append(lambda e, k=k, st=st, pv=pv: e.transpose(
                    pv[:, k * 128:(k + 1) * 128], skh.t[:, st * 128:(st + 1) * 128],
                    identb.t[:]))
            mm(bres, steps, r=(skh.r, identb.r))
            comp("scalar", lambda e, pv=pv, half=half: e.activation(
                out=kph.t[:, half * 1024:(half + 1) * 1024], in_=pv, func=AF.Copy),
                w=(kph.r,), r=tuple(bres))

        st_ps = ps_a[:, 0:132]
        st_bk = banks(ps_a, 0, 132)
        sc_ps = ps_a[:, 1024:1408]
        sc_bk = banks(ps_a, 1024, 1408)
        den_ps = ps_a[0:1, 512:768]
        den_bk = banks(ps_a, 512, 768)
        num_bk = b_all

        for sc in range(NSC):
            st0, st1 = 2 * sc, 2 * sc + 1
            stb = stsb[sc % 2]
            scbb = scb[sc % 2]
            if sc > 0:
                comp("scalar", lambda e, stb=stb: e.activation(
                    out=stb.t[:], in_=st_ps, func=AF.Copy),
                    w=(stb.r,), r=st_bk)
            steps = []
            for cb in (st0, st1):
                first = (sc == 0 and cb == st0)
                steps.append(lambda e, cb=cb, first=first: e.matmul(
                    st_ps[:, 0:128], kph.t[:, cb * 128:(cb + 1) * 128],
                    vh.t[:, cb * 128:(cb + 1) * 128], start=first,
                    stop=(sc == NSC - 1 and cb == st1)))
                steps.append(lambda e, cb=cb: e.matmul(
                    st_ps[:, 128:129], kph.t[:, cb * 128:(cb + 1) * 128],
                    onesb.t[:, 0:1], start=False, stop=(sc == NSC - 1 and cb == st1)))
            stread = [stb.r] if sc > 0 else []
            mm(st_bk, steps, r=[kph.r, vh.r, onesb.r] + stread)
            pairs = [(st0, st0), (st0, st1), (st1, st1)]
            steps = []
            for pi, (jb, ib) in enumerate(pairs):
                steps.append(lambda e, pi=pi, jb=jb, ib=ib: e.matmul(
                    sc_ps[:, pi * 128:(pi + 1) * 128], skh.t[:, jb * 128:(jb + 1) * 128],
                    sqh.t[:, ib * 128:(ib + 1) * 128], start=True, stop=True))
            mm(sc_bk, steps, r=(skh.r, sqh.r))
            comp("vector", lambda e, scbb=scbb: e.tensor_tensor(
                out=scbb.t[:], in0=sc_ps, in1=mask3.t[:], op=ALU.mult),
                w=(scbb.r,), r=sc_bk + [mask3.r])
            specs = []
            for pi, (jb, ib) in enumerate(pairs):
                specs.append(("ssum", pi, (ib - st0) * 128))
            if sc > 0:
                for cb in (st0, st1):
                    specs.append(("zden", cb, (cb - st0) * 128))
            steps = []
            seen = set()
            for i, (kind, aa, col) in enumerate(specs):
                stp = (i == len(specs) - 1)
                first = col not in seen and not seen
                seen.add(col)
                strt = (i == 0)
                if kind == "ssum":
                    steps.append(lambda e, pi=aa, col=col, stp=stp, strt=strt, scbb=scbb: e.matmul(
                        den_ps[:, col:col + 128], onesb.t[:, 0:1],
                        scbb.t[:, pi * 128:(pi + 1) * 128], start=strt, stop=stp))
                else:
                    steps.append(lambda e, cb=aa, col=col, stp=stp, stb=stb: e.matmul(
                        den_ps[:, col:col + 128], stb.t[:, 128:129],
                        sqh.t[:, cb * 128:(cb + 1) * 128], start=False, stop=stp))
            mm(den_bk, steps, r=[onesb.r, scbb.r, sqh.r] + stread)
            comp("vector", lambda e, sc=sc: e.tensor_copy(
                out=fp[0].t[0:1, sc * 256:(sc + 1) * 256], in_=den_ps[:, 0:256]),
                w=(fp[0].r,), r=den_bk)
            for ib in (st0, st1):
                reg = ps_b[:, ib * 128:(ib + 1) * 128]
                regb = banks(ps_b, ib * 128, (ib + 1) * 128)
                jbs = [jb for jb in (st0, st1) if jb <= ib]
                nsteps = len(jbs) + (1 if sc > 0 else 0)
                steps = []
                for k, jb in enumerate(jbs):
                    pi = pairs.index((jb, ib))
                    steps.append(lambda e, reg=reg, jb=jb, pi=pi, k=k, scbb=scbb,
                                 stp=(k == nsteps - 1): e.matmul(
                        reg, vh.t[:, jb * 128:(jb + 1) * 128],
                        scbb.t[:, pi * 128:(pi + 1) * 128],
                        start=(k == 0), stop=stp))
                if sc > 0:
                    steps.append(lambda e, reg=reg, ib=ib, stb=stb: e.matmul(
                        reg, stb.t[:, 0:128], sqh.t[:, ib * 128:(ib + 1) * 128],
                        start=False, stop=True))
                mm(regb, steps, r=[vh.r, scbb.r, sqh.r] + stread)
        comp("vector", lambda e: e.tensor_copy(out=locb.t[:], in_=ps_b[:, 0:2048]),
             w=(locb.r,), r=num_bk)
        dma("sync", at_scr[h], locb.t[:], w=(res_at,), r=(locb.r,))
        dma("sync", den_scr[h], fp[0].t[0:1, :], w=(res_den,), r=(fp[0].r,))
        tot = stsb[0]
        comp("scalar", lambda e, tot=tot: e.activation(out=tot.t[:], in_=st_ps,
                                                       func=AF.Copy),
             w=(tot.r,), r=st_bk)
        dma("sync", st_scr[h * 128:(h + 1) * 128, :], tot.t[:], w=(res_st,), r=(tot.r,))

    comp("gpsimd", lambda e: e.collective_compute(
        "AllGather", ALU.bypass, replica_groups=[list(range(NCORE))],
        ins=[st_scr.ap().opt()], outs=[stg_all.ap().opt()]),
        w=(res_stg,), r=(res_st,), inc=1, incsem="cc")

    steps = []
    for h in range(H):
        steps.append(lambda e, h=h: e.matmul(
            ps_a[0:4, h:h + 1], mnTf.t[:, h * 4:(h + 1) * 4], qsum.t[:, h:h + 1],
            start=True, stop=True))
    mm(banks(ps_a, 0, 16), steps, r=(mnTf.r, qsum.r))
    comp("vector", lambda e: e.tensor_scalar(out=zbuf.t[0:4, :], in0=ps_a[0:4, 0:16],
                                             scalar1=RELSC, scalar2=None, op0=ALU.mult),
         w=(zbuf.r,), r=banks(ps_a, 0, 16))
    if DEBUG:
        dma("sync", zbdbg, zbuf.t[:], r=(zbuf.r,))
    dma("gpsimd", cc_in.ap(), zbuf.t[:], w=(res_ccin,), r=(zbuf.r,))
    comp("gpsimd", lambda e: e.collective_compute(
        "AllReduce", ALU.add, replica_groups=[list(range(NCORE))],
        ins=[cc_in.ap().opt()], outs=[cc_out.ap().opt()]),
        w=(res_ccout,), r=(res_ccin,), inc=1, incsem="cc")
    dma("gpsimd", relg.t[:], cc_out.ap(), w=(relg.r,), r=(res_ccout,),
        xw=(("cc", 2),))
    comp("vector", lambda e: e.tensor_scalar(out=relg.t[0:4, :], in0=relg.t[0:4, :],
                                             scalar1=sm4.t[0:4, 2:3], scalar2=None,
                                             op0=ALU.add),
         w=(relg.r,), r=(relg.r, sm4.r))
    mm(banks(ps_a, 16, 32),
       [lambda e: e.transpose(ps_a[0:16, 16:20], relg.t[0:4, 0:16], identf.t[0:4, 0:4])],
       r=(relg.r, identf.r))
    comp("vector", lambda e: e.tensor_copy(out=relT.t[0:16, 0:4], in_=ps_a[0:16, 16:20]),
         w=(relT.r,), r=banks(ps_a, 16, 32))
    comp("vector", lambda e: e.reduce_max(out=m1c.t[:, 0:1], in_=relT.t[0:16, 0:4],
                                          axis=AX.X),
         w=(m1c.r,), r=(relT.r,))
    comp("vector", lambda e: e.tensor_scalar(out=m1c.t[:, 1:2], in0=m1c.t[:, 0:1],
                                             scalar1=-1.0, scalar2=None, op0=ALU.mult),
         w=(m1c.r,), r=(m1c.r,))
    comp("scalar", lambda e: e.activation(out=relT.t[0:16, 4:8], in_=relT.t[0:16, 0:4],
                                          func=AF.Exp, bias=m1c.t[:, 1:2]),
         w=(relT.r,), r=(relT.r, m1c.r))
    comp("vector", lambda e: e.reduce_sum(out=m1c.t[:, 2:3], in_=relT.t[0:16, 4:8],
                                          axis=AX.X),
         w=(m1c.r,), r=(relT.r,))
    comp("vector", lambda e: e.reciprocal(out=m1c.t[:, 3:4], in_=m1c.t[:, 2:3]),
         w=(m1c.r,), r=(m1c.r,))
    comp("vector", lambda e: e.tensor_scalar(out=wgT.t[:], in0=relT.t[0:16, 4:8],
                                             scalar1=m1c.t[:, 3:4], scalar2=nta16.t[:],
                                             op0=ALU.mult, op1=ALU.mult),
         w=(wgT.r,), r=(relT.r, m1c.r, nta16.r))
    comp("vector", lambda e: e.tensor_scalar(out=wgT.t[:], in0=wgT.t[:],
                                             scalar1=gTc.t[:], scalar2=None, op0=ALU.mult),
         w=(wgT.r,), r=(wgT.r, gTc.r))
    mm(banks(ps_a, 32, 48),
       [lambda e: e.transpose(ps_a[0:4, 32:48], wgT.t[:], identf.t[0:16, 0:16])],
       r=(wgT.r, identf.r))
    comp("vector", lambda e: e.tensor_copy(out=wgmh.t[:], in_=ps_a[0:4, 32:48]),
         w=(wgmh.r,), r=banks(ps_a, 32, 48))

    if DEBUG:
        comp("vector", lambda e: e.tensor_copy(out=lg.t[:, 0:16], in_=qsum.t[:]),
             w=(lg.r,), r=(qsum.r,))
        comp("vector", lambda e: e.tensor_copy(out=lg.t[:, 16:32], in_=relg.t[:]),
             r=(relg.r,))
        comp("vector", lambda e: e.tensor_copy(out=r1.t[0:4, 0:8], in_=wgmh.t[:, 0:8]),
             w=(r1.r,), r=(wgmh.r,))
        dma("sync", smdbg[:, 0:16], lg.t[:, 0:16], r=(lg.r,))
        dma("sync", smdbg[:, 16:32], lg.t[:, 16:32], r=(lg.r,))
        dma("sync", smdbg[0:4, 32:40], r1.t[0:4, 0:8], r=(r1.r,))
        dma("sync", smdbg[:, 48:64], g1bc.t[:], r=(g1bc.r,))
        dma("sync", smdbg[0:4, 40:48], sm4.t[:], r=(sm4.r,))
        dma("sync", smdbg[0:1, 32:34], nta.t[:], r=(nta.r,))

    sqh2, locb2, rmb, sqm, t1b4 = bp[0], bp[1], bp[2], bp[3], bp[4]
    for h in range(H):
        msb = msl[h % 2]
        src = memr_d[h].rearrange("m p j -> p m j")
        dma("gpsimd", msb.t[:].rearrange("p (m j) -> p m j", j=128), src, w=(msb.r,))
        dma("sync", sqh2.t[:], sq_scr[h], w=(sqh2.r,), r=(res_sq,))
        dma("sync", locb2.t[:], at_scr[h], w=(locb2.r,), r=(res_at,))
        dma("sync", fp[2].t[0:1, :], den_scr[h], w=(fp[2].r,), r=(res_den,))
        gsrc = stg_all.ap().rearrange("(c r) f -> c r f", r=H * HD)[
            :, h * 128:(h + 1) * 128, :].rearrange("c p f -> p c f")
        dma("sync", bp[5].t[:, 0:NCORE * 132].rearrange("p (c f) -> p c f", f=132),
            gsrc, w=(bp[5].r,), r=(res_stg,), xw=(("cc", 1),))
        steps = []
        for nb in range(4):
            steps.append(lambda e, nb=nb: e.matmul(
                ps_a[0:4, nb * 512:(nb + 1) * 512], epsr.t[0:1, 0:4],
                onesb.t[0:1, 0:512], start=True, stop=False))
        for nb in range(4):
            steps.append(lambda e, nb=nb, h=h: e.matmul(
                ps_a[0:4, nb * 512:(nb + 1) * 512], mnTb.t[:, h * 4:(h + 1) * 4],
                sqh2.t[:, nb * 512:(nb + 1) * 512], start=False, stop=True))
        mm(a_all, steps, r=(epsr.r, onesb.r, mnTb.r, sqh2.r))
        comp("vector", lambda e: e.reciprocal(out=fp[0].t[0:4, :],
                                              in_=ps_a[0:4, 0:2048]),
             w=(fp[0].r,), r=a_all)
        comp("vector", lambda e, h=h: e.tensor_scalar(
            out=rmb.t[0:4, :], in0=fp[0].t[0:4, :], scalar1=wgmh.t[:, h:h + 1],
            scalar2=None, op0=ALU.mult),
            w=(rmb.r,), r=(fp[0].r, wgmh.r))
        comp("vector", lambda e: e.tensor_tensor(
            out=bp[5].t[:, 0:NCORE * 132].rearrange("p (c f) -> p c f", f=132),
            in0=bp[5].t[:, 0:NCORE * 132].rearrange("p (c f) -> p c f", f=132),
            in1=pmkbc.t[:].unsqueeze(2).broadcast_to([128, NCORE, 132]),
            op=ALU.mult),
            w=(bp[5].r,), r=(bp[5].r, pmkbc.r))
        comp("vector", lambda e: e.tensor_reduce(
            out=fp[1].t[:, 0:132],
            in_=bp[5].t[:, 0:NCORE * 132].rearrange("p (c f) -> p f c", c=NCORE),
            axis=AX.X, op=ALU.add),
            w=(fp[1].r,), r=(bp[5].r,))
        comp("scalar", lambda e: e.copy(out=bp[5].t[:, 1056:1188],
                                        in_=fp[1].t[:, 0:132]),
             w=(bp[5].r,), r=(fp[1].r,))
        sg_kv = bp[5].t[:, 1056:1184]
        sg_z = bp[5].t[:, 1184:1185]
        steps = []
        for nb in range(4):
            steps.append(lambda e, nb=nb, sg_z=sg_z: e.matmul(
                ps_a[0:1, nb * 512:(nb + 1) * 512], sg_z,
                sqh2.t[:, nb * 512:(nb + 1) * 512], start=True, stop=True))
        mm(a_all, steps, r=(bp[5].r, sqh2.r))
        comp("vector", lambda e: e.scalar_tensor_tensor(
            out=fp[2].t[0:1, :], in0=ps_a[0:1, 0:2048], scalar=EPS_ATT,
            in1=fp[2].t[0:1, :], op0=ALU.add, op1=ALU.add),
            w=(fp[2].r,), r=a_all + [fp[2].r])
        comp("vector", lambda e: e.reciprocal(out=fp[1].t[0:1, :],
                                              in_=fp[2].t[0:1, :]),
             w=(fp[1].r,), r=(fp[2].r,))
        for half in range(2):
            bcb = banks(ps_a, 0, 1024)
            steps = []
            for nb in range(2):
                steps.append(lambda e, half=half, nb=nb: e.matmul(
                    ps_a[:, nb * 512:(nb + 1) * 512], onesfr.t[:],
                    fp[1].t[0:1, half * 1024 + nb * 512: half * 1024 + (nb + 1) * 512],
                    start=True, stop=True))
            mm(bcb, steps, r=(onesfr.r, fp[1].r))
            comp("scalar", lambda e, half=half: e.copy(
                out=sqm.t[:, half * 1024:(half + 1) * 1024], in_=ps_a[:, 0:1024]),
                w=(sqm.r,), r=bcb)
        steps = []
        for nb in range(4):
            steps.append(lambda e, nb=nb, sg_kv=sg_kv: e.matmul(
                ps_b[:, nb * 512:(nb + 1) * 512], sg_kv,
                sqh2.t[:, nb * 512:(nb + 1) * 512], start=True, stop=True))
        mm(b_all, steps, r=(bp[5].r, sqh2.r))
        comp("vector", lambda e: e.tensor_tensor(out=t1b4.t[:], in0=ps_b[:, 0:2048],
                                                 in1=locb2.t[:], op=ALU.add),
             w=(t1b4.r,), r=b_all + [locb2.r])
        comp("vector", lambda e, h=h: e.scalar_tensor_tensor(
            out=locb2.t[:], in0=t1b4.t[:], scalar=g1bc.t[:, h:h + 1],
            in1=sqm.t[:], op0=ALU.mult, op1=ALU.mult),
            w=(locb2.r,), r=(t1b4.r, g1bc.r, sqm.r))
        for m in range(M):
            bcb = a_all
            steps = []
            for nb in range(4):
                steps.append(lambda e, m=m, nb=nb: e.matmul(
                    ps_a[:, nb * 512:(nb + 1) * 512], selmat.t[0:4, m * 128:(m + 1) * 128],
                    rmb.t[0:4, nb * 512:(nb + 1) * 512], start=True, stop=True))
            mm(bcb, steps, r=(selmat.r, rmb.r))
            if m < 2:
                comp("vector", lambda e: e.tensor_tensor(out=sqm.t[:], in0=sqh2.t[:],
                                                         in1=ps_a[:, 0:2048], op=ALU.mult),
                     w=(sqm.r,), r=[sqh2.r] + a_all)
            else:
                comp("scalar", lambda e: e.copy(out=t1b4.t[:], in_=ps_a[:, 0:2048]),
                     w=(t1b4.r,), r=a_all)
                comp("gpsimd", lambda e: e.tensor_tensor(out=sqm.t[:], in0=sqh2.t[:],
                                                         in1=t1b4.t[:], op=ALU.mult),
                     w=(sqm.r,), r=(sqh2.r, t1b4.r))
            steps = []
            for nb in range(4):
                steps.append(lambda e, m=m, nb=nb, msb=msb: e.matmul(
                    ps_b[:, nb * 512:(nb + 1) * 512], msb.t[:, m * 128:(m + 1) * 128],
                    sqm.t[:, nb * 512:(nb + 1) * 512], start=(m == 0), stop=(m == M - 1)))
            mm(b_all, steps, r=(msb.r, sqm.r))
        comp("vector", lambda e: e.tensor_tensor(out=t1b4.t[:], in0=ps_b[:, 0:2048],
                                                 in1=locb2.t[:], op=ALU.add),
             w=(t1b4.r,), r=b_all + [locb2.r])
        dma("sync", at_scr[h], t1b4.t[:], w=(res_at,), r=(t1b4.r,))

    for dc in range(NDC):
        src = woT_d[dc * 128:(dc + 1) * 128, :]
        dma("gpsimd", arenaB.t[:, dc * T:(dc + 1) * T], src, w=(arenaB.r,))

    for st in range(NT):
        at = bp[st % 2]
        fb = fp[st % 2]
        src = at_scr[:, :, st * 128:(st + 1) * 128].rearrange("h p j -> p h j")
        dma("sync", at.t[:].rearrange("p (h j) -> p h j", j=128), src, w=(at.r,), r=(res_at,))
        dma("sync", fb.t[:], x_d[st * 128:(st + 1) * 128, :], w=(fb.r,))
        pst = ps_a if st % 2 == 0 else ps_b
        pbk = a_all if st % 2 == 0 else b_all
        steps = []
        for dc in range(NDC):
            for nb in range(4):
                steps.append(lambda e, dc=dc, nb=nb, at=at, pst=pst: e.matmul(
                    pst[:, nb * 512:(nb + 1) * 512], at.t[:, dc * 128:(dc + 1) * 128],
                    arenaB.t[:, dc * T + nb * 512: dc * T + (nb + 1) * 512],
                    start=(dc == 0), stop=(dc == NDC - 1)))
        mm(pbk, steps, r=(at.r, arenaB.r))
        comp("vector", lambda e, fb=fb, pst=pst: e.tensor_tensor(
            out=fp[2].t[:], in0=pst[:, 0:2048], in1=fb.t[:], op=ALU.add),
            w=(fp[2].r,), r=pbk + [fb.r])
        comp("vector", lambda e, st=st: e.reduce_sum(out=s1a.t[:, st:st + 1],
                                                     in_=fp[2].t[:], axis=AX.X),
             w=(s1a.r,), r=(fp[2].r,))
        comp("scalar", lambda e, st=st, fb=fb: e.activation(
            out=fb.t[:], in_=fp[2].t[:], func=AF.Square, accum_out=ssqa.t[:, st:st + 1]),
            w=(fb.r, ssqa.r), r=(fp[2].r,))
        dma("sync", xp_scr[st], fp[2].t[:], w=(res_xp,), r=(fp[2].r,))

    comp("vector", lambda e: e.tensor_scalar(out=mean_ap, in0=s1a.t[:], scalar1=1.0 / D,
                                             scalar2=None, op0=ALU.mult),
         w=(lntmp.r,), r=(s1a.r,))
    comp("vector", lambda e: e.tensor_scalar(out=t1_ap, in0=ssqa.t[:], scalar1=1.0 / D,
                                             scalar2=None, op0=ALU.mult),
         w=(lntmp.r,), r=(ssqa.r, lntmp.r))
    comp("vector", lambda e: e.tensor_tensor(out=lntmp.t[:, 32:48], in0=mean_ap,
                                             in1=mean_ap, op=ALU.mult),
         w=(lntmp.r,), r=(lntmp.r,))
    comp("vector", lambda e: e.tensor_tensor(out=t1_ap, in0=t1_ap,
                                             in1=lntmp.t[:, 32:48], op=ALU.subtract),
         w=(lntmp.r,), r=(lntmp.r,))
    comp("vector", lambda e: e.tensor_scalar(out=t1_ap, in0=t1_ap, scalar1=EPS_LN,
                                             scalar2=None, op0=ALU.add),
         w=(lntmp.r,), r=(lntmp.r,))
    comp("scalar", lambda e: e.activation(out=lntmp.t[:, 32:48], in_=t1_ap, func=AF.Sqrt),
         w=(lntmp.r,), r=(lntmp.r,))
    comp("vector", lambda e: e.reciprocal(out=rstda.t[:], in_=lntmp.t[:, 32:48]),
         w=(rstda.r,), r=(lntmp.r,))
    comp("vector", lambda e: e.scalar_tensor_tensor(out=ngmra.t[:], in0=mean_ap,
                                                    scalar=-1.0, in1=rstda.t[:],
                                                    op0=ALU.mult, op1=ALU.mult),
         w=(ngmra.r,), r=(lntmp.r, rstda.r))

    gaA = wsl[0]
    dma("gpsimd", gaG.t[:].rearrange("p (dc j) -> p dc j", j=8),
        G_d.rearrange("(dc p) j -> p dc j", p=128), w=(gaG.r,))
    dma("gpsimd", gaA.t[:].rearrange("p (dc j) -> p dc j", j=128),
        laT_d.rearrange("(dc p) j -> p dc j", p=128), w=(gaA.r,))

    for st in range(NT):
        fb = fp[st % 2]
        x2 = bp[2 + st % 2]
        dma("sync", fb.t[:], xp_scr[st], w=(fb.r,), r=(res_xp,))
        comp("gpsimd", lambda e, st=st, fb=fb, x2=x2: e.tensor_scalar(
            out=x2.t[:], in0=fb.t[:], scalar1=rstda.t[:, st:st + 1],
            scalar2=ngmra.t[:, st:st + 1], op0=ALU.mult, op1=ALU.add),
            w=(x2.r,), r=(fb.r, rstda.r, ngmra.r))
        for half in range(2):
            pv = ps_a[:, half * 512:(half + 1) * 512].bitcast(BF16)
            bres = banks(ps_a, half * 512, (half + 1) * 512)
            steps = []
            for k in range(8):
                dc = half * 8 + k
                steps.append(lambda e, k=k, dc=dc, pv=pv, x2=x2: e.transpose(
                    pv[:, k * 128:(k + 1) * 128], x2.t[:, dc * 128:(dc + 1) * 128],
                    identb.t[:]))
            mm(bres, steps, r=(x2.r, identb.r))
            outap = arenaA.t[:].rearrange("p (dc s) -> p dc s", s=T)[
                :, half * 8:(half + 1) * 8, st * 128:(st + 1) * 128]
            comp("scalar", lambda e, pv=pv, outap=outap: e.activation(
                out=outap, in_=pv.rearrange("p (dc j) -> p dc j", j=128), func=AF.Copy),
                w=(arenaA.r,), r=tuple(bres))
        rp = ps_b[:, 0:136]
        rbk = banks(ps_b, 0, 136)
        steps = []
        for dc in range(NDC):
            steps.append(lambda e, dc=dc, st=st: e.matmul(
                rp[:, 0:8], arenaA.t[:, dc * T + st * 128: dc * T + (st + 1) * 128],
                gaG.t[:, dc * 8:(dc + 1) * 8], start=(dc == 0), stop=(dc == NDC - 1)))
            steps.append(lambda e, dc=dc, st=st: e.matmul(
                rp[:, 8:136], arenaA.t[:, dc * T + st * 128: dc * T + (st + 1) * 128],
                gaA.t[:, dc * 128:(dc + 1) * 128], start=False, stop=(dc == NDC - 1)))
        mm(rbk, steps, r=(arenaA.r, gaG.r, gaA.r))
        comp("vector", lambda e: e.tensor_copy(out=lg.t[:, 0:8], in_=rp[:, 0:8]),
             w=(lg.r,), r=rbk)
        comp("vector", lambda e: e.reduce_max(out=r1.t[:, 0:1], in_=lg.t[:, 0:8], axis=AX.X),
             w=(r1.r,), r=(lg.r,))
        comp("vector", lambda e: e.tensor_scalar(out=r1.t[:, 1:2], in0=r1.t[:, 0:1],
                                                 scalar1=-1.0, scalar2=None, op0=ALU.mult),
             w=(r1.r,), r=(r1.r,))
        comp("scalar", lambda e: e.activation(out=lg.t[:, 8:16], in_=lg.t[:, 0:8],
                                              func=AF.Exp, bias=r1.t[:, 1:2]),
             w=(lg.r,), r=(lg.r, r1.r))
        comp("vector", lambda e: e.reduce_sum(out=r1.t[:, 2:3], in_=lg.t[:, 8:16], axis=AX.X),
             w=(r1.r,), r=(lg.r,))
        comp("vector", lambda e: e.reciprocal(out=r1.t[:, 3:4], in_=r1.t[:, 2:3]),
             w=(r1.r,), r=(r1.r,))
        comp("vector", lambda e: e.tensor_scalar(out=lg.t[:, 16:24], in0=lg.t[:, 8:16],
                                                 scalar1=r1.t[:, 3:4], scalar2=None,
                                                 op0=ALU.mult),
             w=(lg.r,), r=(lg.r, r1.r))
        comp("vector", lambda e: e.reduce_max(out=r1.t[:, 4:5], in_=lg.t[:, 16:24], axis=AX.X),
             w=(r1.r,), r=(lg.r,))
        comp("vector", lambda e: e.tensor_scalar(out=lg.t[:, 24:32], in0=lg.t[:, 16:24],
                                                 scalar1=r1.t[:, 4:5], scalar2=None,
                                                 op0=ALU.is_lt),
             w=(lg.r,), r=(lg.r, r1.r))
        comp("vector", lambda e: e.tensor_tensor(out=lg.t[:, 8:16], in0=lg.t[:, 16:24],
                                                 in1=lg.t[:, 24:32], op=ALU.mult),
             w=(lg.r,), r=(lg.r,))
        comp("vector", lambda e: e.reduce_max(out=r1.t[:, 5:6], in_=lg.t[:, 8:16], axis=AX.X),
             w=(r1.r,), r=(lg.r,))
        comp("vector", lambda e: e.tensor_scalar(out=lg.t[:, 24:32], in0=lg.t[:, 16:24],
                                                 scalar1=r1.t[:, 5:6], scalar2=None,
                                                 op0=ALU.is_ge),
             w=(lg.r,), r=(lg.r, r1.r))
        comp("vector", lambda e, st=st: e.tensor_tensor(
            out=wall.t[:, st * 8:(st + 1) * 8], in0=lg.t[:, 16:24], in1=lg.t[:, 24:32],
            op=ALU.mult),
            w=(wall.r,), r=(lg.r,))
        if DEBUG and st == 0:
            dma("sync", rtdbg[:, 0:32], lg.t[:], r=(lg.r,))
            dma("sync", rtdbg[:, 32:40], r1.t[:], r=(r1.r,))
        comp("vector", lambda e, st=st: e.tensor_copy(
            out=wexp.t[:].rearrange("p (e r) -> p e r", r=16),
            in_=wall.t[:, st * 8:(st + 1) * 8].unsqueeze(2).broadcast_to([128, 8, 16])),
            w=(wexp.r,), r=(wall.r,))
        lrs = bp[4]
        comp("vector", lambda e: e.tensor_tensor(out=lrs.t[:, 0:128], in0=rp[:, 8:136],
                                                 in1=wexp.t[:], op=ALU.mult),
             w=(lrs.r,), r=rbk + [wexp.r])
        pv = ps_b[:, 512:576].bitcast(BF16)
        bres = banks(ps_b, 512, 576)
        mm(bres, [lambda e, pv=pv, lrs=lrs: e.transpose(pv[:, 0:128], lrs.t[:, 0:128],
                                                        identb.t[:])],
           r=(lrs.r, identb.r))
        comp("scalar", lambda e, st=st, pv=pv: e.activation(
            out=lrw.t[:, st * 128:(st + 1) * 128], in_=pv[:, 0:128], func=AF.Copy),
            w=(lrw.r,), r=tuple(bres))
        pw = ps_b[0:8, 640:768]
        bres2 = banks(ps_b, 640, 768)
        mm(bres2, [lambda e, st=st, pw=pw: e.transpose(
            pw, wall.t[:, st * 8:(st + 1) * 8], identf.t[:])],
           r=(wall.r, identf.r))
        comp("scalar", lambda e, st=st, pw=pw: e.activation(
            out=wT8.t[0:8, st * 128:(st + 1) * 128], in_=pw, func=AF.Copy),
            w=(wT8.r,), r=tuple(bres2))

    if DEBUG:
        dma("sync", x2dbg, arenaA.t[:], r=(arenaA.r,))
        dma("sync", wldbg, wall.t[:], r=(wall.r,))

    t1b, t2b = bp[0], bp[1]
    for fc in range(NDC):
        slg, slu = wsl[(2 * fc) % 4], wsl[(2 * fc + 1) % 4]
        load_wslab(slg, gupg_d, fc)
        load_wslab(slu, gupu_d, fc)
        e_x = fc // 2
        for sh in range(2):
            par = (fc * 2 + sh) % 2
            pst = ps_a if par == 0 else ps_b
            pbk4 = a_all if par == 0 else b_all
            opst = ps_b if par == 0 else ps_a
            opbk = banks(ps_b, 0, 512) if par == 0 else banks(ps_a, 0, 512)
            if sh == 0 and fc % 2 == 0:
                for nb in range(4):
                    mm(opbk, [lambda e, e_x=e_x, nb=nb, opst=opst: e.matmul(
                        opst[:, 0:512], selmat.t[:, e_x * 128:(e_x + 1) * 128],
                        wT8.t[0:8, nb * 512:(nb + 1) * 512], start=True, stop=True)],
                       r=(selmat.r, wT8.r))
                    comp("vector", lambda e, nb=nb, opst=opst: e.tensor_copy(
                        out=wbc.t[:, nb * 512:(nb + 1) * 512], in_=opst[:, 0:512]),
                        w=(wbc.r,), r=opbk)
            steps = []
            for dc in range(NDC):
                for nb in range(2):
                    steps.append(lambda e, dc=dc, nb=nb, sh=sh, slg=slg, pst=pst: e.matmul(
                        pst[:, nb * 512:(nb + 1) * 512], slg.t[:, dc * 128:(dc + 1) * 128],
                        arenaA.t[:, dc * T + sh * 1024 + nb * 512:
                                 dc * T + sh * 1024 + (nb + 1) * 512],
                        start=(dc == 0), stop=(dc == NDC - 1)))
            mm(pbk4[0:2], steps, r=(slg.r, arenaA.r))
            steps = []
            for dc in range(NDC):
                for nb in range(2):
                    steps.append(lambda e, dc=dc, nb=nb, sh=sh, slu=slu, pst=pst: e.matmul(
                        pst[:, 1024 + nb * 512:1024 + (nb + 1) * 512],
                        slu.t[:, dc * 128:(dc + 1) * 128],
                        arenaA.t[:, dc * T + sh * 1024 + nb * 512:
                                 dc * T + sh * 1024 + (nb + 1) * 512],
                        start=(dc == 0), stop=(dc == NDC - 1)))
            mm(pbk4[2:4], steps, r=(slu.r, arenaA.r))
            comp("scalar", lambda e, pst=pst: e.activation(out=t1b.t[:, 0:1024],
                                                           in_=pst[:, 0:1024], func=AF.Silu),
                 w=(t1b.r,), r=pbk4[0:2])
            comp("vector", lambda e, pst=pst: e.tensor_tensor(
                out=t2b.t[:, 0:1024], in0=t1b.t[:, 0:1024], in1=pst[:, 1024:2048],
                op=ALU.mult),
                w=(t2b.r,), r=[t1b.r] + pbk4[2:4])
            comp("vector", lambda e, fc=fc, sh=sh: e.tensor_tensor(
                out=arenaB.t[:, fc * T + sh * 1024: fc * T + (sh + 1) * 1024],
                in0=t2b.t[:, 0:1024], in1=wbc.t[:, sh * 1024:(sh + 1) * 1024],
                op=ALU.mult),
                w=(arenaB.r,), r=(t2b.r, wbc.r))

    if DEBUG:
        dma("sync", gadbg, arenaB.t[:], r=(arenaB.r,))

    for dc in range(NDC):
        dma("gpsimd", arenaA.t[:, dc * T:(dc + 1) * T],
            downT_d[dc * 128:(dc + 1) * 128, :], w=(arenaA.r,))
    lbTs = wsl[3]
    dma("gpsimd", lbTs.t[:], lbT_d, w=(lbTs.r,))

    for st in range(NT):
        fb = fp[st % 2]
        dma("sync", fb.t[:], xp_scr[st], w=(fb.r,), r=(res_xp,))
        pst = ps_a if st % 2 == 0 else ps_b
        pbk = a_all if st % 2 == 0 else b_all
        steps = []
        for fc in range(NDC + 1):
            for nb in range(4):
                if fc < NDC:
                    steps.append(lambda e, fc=fc, nb=nb, st=st, pst=pst: e.matmul(
                        pst[:, nb * 512:(nb + 1) * 512],
                        arenaB.t[:, fc * T + st * 128: fc * T + (st + 1) * 128],
                        arenaA.t[:, fc * T + nb * 512: fc * T + (nb + 1) * 512],
                        start=(fc == 0), stop=False))
                else:
                    steps.append(lambda e, nb=nb, st=st, pst=pst: e.matmul(
                        pst[:, nb * 512:(nb + 1) * 512],
                        lrw.t[:, st * 128:(st + 1) * 128],
                        lbTs.t[:, nb * 512:(nb + 1) * 512],
                        start=False, stop=True))
        mm(pbk, steps, r=(arenaA.r, arenaB.r, lrw.r, lbTs.r))
        comp("vector", lambda e, fb=fb, pst=pst: e.tensor_tensor(
            out=fp[2].t[:], in0=pst[:, 0:2048], in1=fb.t[:], op=ALU.add),
            w=(fp[2].r,), r=pbk + [fb.r])
        dma("sync", y_d[st * 128:(st + 1) * 128, :], fp[2].t[:], r=(fp[2].r,))

    final_sy = p.cnt["sy"]

    with nc.allow_low_precision(reason="bf16 matmul operands / bf16 psum transposes are intentional"), nc.Block() as block:
        for eng in ENGS:
            def make(eng=eng):
                def body(e):
                    for ws, fn, sem, inc in p.ops[eng]:
                        for s, v in ws:
                            e.wait_ge(sems[s], v)
                        ins = fn(e)
                        if sem is not None:
                            ins.then_inc(sems[sem], inc)
                    if eng == "sync":
                        e.wait_ge(sems["sy"], final_sy)
                return body
            getattr(block, eng)(make())
    es.close()
    return nc


_NC_CACHE = {}


def _get_nc():
    key = (DEBUG,)
    if key not in _NC_CACHE:
        _NC_CACHE[key] = build_program()
    return _NC_CACHE[key]


def _prep(L0_attn_hidden_states, L0_attn_memories, L0_attn_memory_norms,
          L0_attn_w_q, L0_attn_w_k, L0_attn_w_v, L0_attn_w_o, L0_attn_gate,
          L0_ffn_G, L0_ffn_gate_w, L0_ffn_up_w, L0_ffn_down_w,
          L0_ffn_lora_A, L0_ffn_lora_B):
    f32 = np.float32
    asn = lambda a: np.ascontiguousarray(np.asarray(a, dtype=f32))
    x_full = asn(L0_attn_hidden_states).reshape(-1, D)
    shared = {
        "wqT": asn(np.asarray(L0_attn_w_q, f32).T),
        "wkT": asn(np.asarray(L0_attn_w_k, f32).T),
        "wvT": asn(np.asarray(L0_attn_w_v, f32).T),
        "woT": asn(np.asarray(L0_attn_w_o, f32).T),
        "memr": asn(np.transpose(np.asarray(L0_attn_memories, f32), (1, 0, 2, 3))),
        "mnT": asn(np.transpose(np.asarray(L0_attn_memory_norms, f32), (2, 1, 0)).reshape(HD, H * M)),
        "mnF": asn(np.asarray(L0_attn_memory_norms, f32).reshape(M, H * HD)),
        "gatev": asn(np.asarray(L0_attn_gate, f32).reshape(1, H)),
        "G": asn(L0_ffn_G),
        "laT": asn(np.transpose(np.asarray(L0_ffn_lora_A, f32), (2, 0, 1)).reshape(D, 128)),
        "gupTg": asn(np.transpose(np.asarray(L0_ffn_gate_w, f32), (2, 0, 1)).reshape(D, D)),
        "gupTu": asn(np.transpose(np.asarray(L0_ffn_up_w, f32), (2, 0, 1)).reshape(D, D)),
        "downT": asn(np.transpose(np.asarray(L0_ffn_down_w, f32), (0, 2, 1)).reshape(D, D)),
        "lbT": asn(np.transpose(np.asarray(L0_ffn_lora_B, f32), (0, 2, 1)).reshape(128, D)),
    }
    in_maps = []
    for c in range(NCORE):
        m = dict(shared)
        m["x"] = np.ascontiguousarray(x_full[c * T:(c + 1) * T, :])
        m["pmask"] = np.ascontiguousarray(
            (np.arange(NCORE) < c).astype(f32).reshape(1, NCORE))
        in_maps.append(m)
    return in_maps


def kernel(**inputs):
    in_maps = _prep(**inputs)
    nc = _get_nc()
    res = run_bass_kernel_spmd(nc, in_maps, core_ids=list(range(NCORE)))
    out = np.concatenate([np.asarray(r["y"], dtype=np.float32) for r in res.results],
                         axis=0)
    return out.reshape(1, NCORE * T, D)


# version 06
# speedup vs baseline: 1.1035x; 1.1035x over previous
import os
import numpy as np
import ml_dtypes
from contextlib import ExitStack

import concourse.bass as bass
import concourse.mybir as mybir
from concourse.bass_utils import run_bass_kernel_spmd

F32 = mybir.dt.float32
BF16 = mybir.dt.bfloat16
AF = mybir.ActivationFunctionType
ALU = mybir.AluOpType
AX = mybir.AxisListType
bf16 = ml_dtypes.bfloat16

NCORE = 8
T = 2048
D = 2048
NT = 16
NDC = 16
H = 16
HD = 128
M = 4
NSC = 8
EPS_ATT = 1e-6
EPS_LN = 1e-5
RELSC = (HD ** -0.5) / 16384.0

DEBUG = bool(int(os.environ.get("MK_DEBUG", "0")))

ENGS = ("sync", "gpsimd", "tensor", "vector", "scalar")
ESEM = {"sync": "sy", "gpsimd": "gd", "tensor": "pe", "vector": "dv", "scalar": "ac"}


class Res:
    __slots__ = ("writes", "reads", "excl", "name")

    def __init__(self, name="", excl=False):
        self.writes = {}
        self.reads = {}
        self.excl = excl
        self.name = name


class Prog:
    def __init__(self, sems):
        self.sems = sems
        self.ops = {e: [] for e in ENGS}
        self.cnt = {s: 0 for s in sems}
        self.lastwait = {e: {} for e in ENGS}

    def emit(self, eng, fn, waits=(), inc=None, incsem=None):
        sem = incsem or ESEM[eng]
        ws = []
        for s, v in waits:
            if v <= 0:
                continue
            assert v <= self.cnt[s], (
                f"wait-on-future: {eng} waits {s}>={v}, emitted {self.cnt[s]}")
            if s == ESEM[eng] and eng == "tensor":
                continue
            if self.lastwait[eng].get(s, 0) >= v:
                continue
            self.lastwait[eng][s] = v
            ws.append((s, v))
        mil = None
        if inc:
            self.cnt[sem] += inc
            mil = (sem, self.cnt[sem])
        self.ops[eng].append((tuple(ws), fn, sem if inc else None, inc))
        return mil

    def rw(self, res, eng):
        w = list(res.writes.items())
        if res.excl:
            for s, v in res.reads.items():
                if s != ESEM[eng]:
                    w.append((s, v))
        return w

    def ww(self, res, eng):
        return list(res.writes.items()) + list(res.reads.items())

    @staticmethod
    def note_r(res, mil):
        if mil is not None:
            s, v = mil
            if res.reads.get(s, 0) < v:
                res.reads[s] = v

    @staticmethod
    def note_w(res, mil):
        if mil is not None:
            res.writes = {mil[0]: mil[1]}
            res.reads = {}


class Buf:
    __slots__ = ("t", "r")

    def __init__(self, t, name, excl=False):
        self.t = t
        self.r = Res(name, excl)


def build_program():
    nc = bass.Bass("TRN2", target_bir_lowering=False, debug=False)

    def din(name, shape):
        return nc.dram_tensor(name, list(shape), F32, kind="ExternalInput").ap()

    x_d = din("x", (T, D))
    pmask_d = din("pmask", (1, NCORE))
    wqT_d = din("wqT", (D, D))
    wkT_d = din("wkT", (D, D))
    wvT_d = din("wvT", (D, D))
    woT_d = din("woT", (D, D))
    memr_d = din("memr", (H, M, HD, HD))
    mnT_d = din("mnT", (HD, H * M))
    mnF_d = din("mnF", (M, H * HD))
    gate_d = din("gatev", (1, H))
    G_d = din("G", (D, 8))
    laT_d = din("laT", (D, 128))
    gupg_d = din("gupTg", (D, D))
    gupu_d = din("gupTu", (D, D))
    downT_d = din("downT", (D, D))
    lbT_d = din("lbT", (128, D))

    y_d = nc.dram_tensor("y", [T, D], F32, kind="ExternalOutput").ap()

    skind = "ExternalOutput" if DEBUG else "Internal"
    sq_scr = nc.dram_tensor("sq_scr", [H, HD, T], BF16, kind=skind).ap()
    at_scr = nc.dram_tensor("at_scr", [H, HD, T], BF16, kind=skind).ap()
    xp_scr = nc.dram_tensor("xp_scr", [NT, 128, D], F32, kind=skind).ap()
    if DEBUG:
        x1dbg = nc.dram_tensor("x1dbg", [128, NDC * T], BF16, kind="ExternalOutput").ap()
        qdbg = nc.dram_tensor("qdbg", [128, T], F32, kind="ExternalOutput").ap()
        x2dbg = nc.dram_tensor("x2dbg", [128, NDC * T], BF16, kind="ExternalOutput").ap()
        gadbg = nc.dram_tensor("gadbg", [128, NDC * T], BF16, kind="ExternalOutput").ap()
        smdbg = nc.dram_tensor("smdbg", [128, 64], F32, kind="ExternalOutput").ap()
        zbdbg = nc.dram_tensor("zbdbg", [128, 16], F32, kind="ExternalOutput").ap()
        rtdbg = nc.dram_tensor("rtdbg", [128, 40], F32, kind="ExternalOutput").ap()
        wldbg = nc.dram_tensor("wldbg", [128, 128], F32, kind="ExternalOutput").ap()
    cc_in = nc.dram_tensor("cc_in", [128, 16], F32, kind="Internal")
    cc_out = nc.dram_tensor("cc_out", [128, 16], F32, kind="Internal")
    den_scr = nc.dram_tensor("den_scr", [H, 1, T], F32, kind=skind).ap()
    st_scr = nc.dram_tensor("st_scr", [H * HD, 132], BF16, kind="Internal")
    stg_all = nc.dram_tensor("stg_all", [NCORE * H * HD, 132], BF16, kind="Internal")

    identb_h = nc.inline_tensor(np.eye(128, dtype=np.float32).astype(bf16), name="c_identb")
    identf_h = nc.inline_tensor(np.eye(128, dtype=np.float32), name="c_identf")
    triu = np.triu(np.ones((128, 128), np.float32))
    mask3_h = nc.inline_tensor(
        np.concatenate([triu, np.ones((128, 128), np.float32), triu], 1).astype(bf16),
        name="c_mask3")
    onesb_h = nc.inline_tensor(np.ones((128, 512), np.float32).astype(bf16), name="c_onesb")
    onesf_h = nc.inline_tensor(np.ones((128, 16), np.float32), name="c_onesf")
    onesfr_h = nc.inline_tensor(np.ones((1, 128), np.float32), name="c_onesfr")
    epsr_h = nc.inline_tensor(np.full((1, 8), EPS_ATT, np.float32).astype(bf16), name="c_epsr")
    sel_np = np.zeros((8, 8 * 128), np.float32)
    for e in range(8):
        sel_np[e, e * 128:(e + 1) * 128] = 1.0
    selmat_h = nc.inline_tensor(sel_np.astype(bf16), name="c_selmat")


    es = ExitStack()
    ec = es.enter_context

    def S(name, shape, dt):
        return Buf(ec(nc.sbuf_tensor(name, list(shape), dt)), name)

    arenaA = S("arenaA", (128, NDC * T), BF16)
    arenaB = S("arenaB", (128, NDC * T), BF16)
    fp = [S(f"fp{i}", (128, D), F32) for i in range(3)]
    bp = [S(f"bp{i}", (128, D), BF16) for i in range(6)]
    wsl = [S(f"wsl{i}", (128, D), BF16) for i in range(4)]
    msl = [S(f"msl{i}", (128, 512), BF16) for i in range(2)]
    gaG = S("gaG", (128, NDC * 8), BF16)
    lrw = S("lrw", (128, T), BF16)
    scb = [S(f"scb{i}", (128, 384), BF16) for i in range(2)]
    stsb = [S(f"stsb{i}", (128, 132), BF16) for i in range(2)]
    identb = S("identb", (128, 128), BF16)
    identf = S("identf", (128, 128), F32)
    mask3 = S("mask3", (128, 384), BF16)
    onesb = S("onesb", (128, 512), BF16)
    onesf = S("onesf", (128, 16), F32)
    onesfr = S("onesfr", (1, 128), F32)
    epsr = S("epsr", (1, 8), BF16)
    selmat = S("selmat", (8, 8 * 128), BF16)
    mnTf = S("mnTf", (128, H * M), F32)
    mnTb = S("mnTb", (128, H * M), BF16)
    qsum = S("qsum", (128, H), F32)
    gavs = S("gavs", (1, H), F32)
    g1bc = S("g1bc", (128, H), F32)
    gTc = S("gTc", (16, 1), F32)
    sm4 = S("sm4", (4, 8), F32)
    nta = S("nta", (1, 2), F32)
    nta16 = S("nta16", (16, 1), F32)
    relT = S("relT", (16, 8), F32)
    m1c = S("m1c", (16, 4), F32)
    wgT = S("wgT", (16, 4), F32)
    wgmh = S("wgmh", (4, 16), F32)
    zbuf = S("zbuf", (128, 16), F32)
    relg = zbuf
    lntmp = S("lntmp", (128, 48), F32)
    pmk = S("pmk", (1, NCORE), F32)
    pmkbc = S("pmkbc", (128, NCORE), F32)
    s1a = S("s1a", (128, NT), F32)
    ssqa = S("ssqa", (128, NT), F32)
    rstda = S("rstda", (128, NT), F32)
    ngmra = S("ngmra", (128, NT), F32)
    wall = S("wall", (128, NT * 8), F32)
    wbc = bp[2]
    wT8 = bp[5]
    rdrow = lrw
    wexp = S("wexp", (128, 128), BF16)
    lg = S("lg", (128, 32), F32)
    r1 = S("r1", (128, 8), F32)

    ps_a = ec(nc.psum_tensor("ps_a", [128, 2048], F32))
    ps_b = ec(nc.psum_tensor("ps_b", [128, 2048], F32))
    bankres = {("a", i): Res(f"pa{i}", excl=True) for i in range(4)}
    bankres.update({("b", i): Res(f"pb{i}", excl=True) for i in range(4)})

    def banks(t, c0, c1):
        nm = "a" if t is ps_a else "b"
        return [bankres[(nm, i)] for i in range(c0 // 512, (c1 + 511) // 512)]

    res_sq = Res("sq_scr")
    res_at = Res("at_scr")
    res_xp = Res("xp_scr")
    res_ccin = Res("cc_in")
    res_ccout = Res("cc_out")
    res_den = Res("den_scr")
    res_st = Res("st_scr")
    res_stg = Res("stg_all")

    sems = {s: ec(nc.semaphore(s)) for s in ("sy", "gd", "pe", "dv", "ac", "cc")}
    p = Prog(sems)

    def dma(eng, dst, src, w=(), r=(), xw=()):
        waits = list(xw)
        for res in w:
            waits += p.ww(res, eng)
        for res in r:
            waits += p.rw(res, eng)
        mil = p.emit(eng, (lambda e, d=dst, s=src: e.dma_start(out=d, in_=s)),
                     waits=waits, inc=16)
        for res in w:
            Prog.note_w(res, mil)
        for res in r:
            Prog.note_r(res, mil)
        return mil

    def comp(eng, fn, w=(), r=(), xw=(), inc=1, incsem=None):
        waits = list(xw)
        for res in w:
            waits += p.ww(res, eng)
        for res in r:
            waits += p.rw(res, eng)
        mil = p.emit(eng, fn, waits=waits, inc=inc, incsem=incsem)
        for res in w:
            Prog.note_w(res, mil)
        for res in r:
            Prog.note_r(res, mil)
        return mil

    def mm(outbanks, steps, r=()):
        waits = []
        for res in outbanks:
            waits += p.ww(res, "tensor")
        for res in r:
            waits += p.rw(res, "tensor")
        mil = None
        n = len(steps)
        for i, fn in enumerate(steps):
            mil = p.emit("tensor", fn, waits=(waits if i == 0 else ()),
                         inc=(1 if i == n - 1 else None))
        for res in outbanks:
            Prog.note_w(res, mil)
        for res in r:
            Prog.note_r(res, mil)
        return mil

    for h_, b_ in ((identb_h, identb), (identf_h, identf), (mask3_h, mask3),
                   (onesb_h, onesb), (onesf_h, onesf), (onesfr_h, onesfr),
                   (epsr_h, epsr), (selmat_h, selmat)):
        dma("sync", b_.t[:], h_.ap(), w=(b_.r,))
    dma("sync", mnTf.t[:], mnT_d, w=(mnTf.r,))
    dma("gpsimd", mnTb.t[:], mnT_d, w=(mnTb.r,))
    dma("sync", fp[0].t[0:M, :], mnF_d, w=(fp[0].r,))
    dma("sync", gavs.t[:], gate_d, w=(gavs.r,))
    dma("sync", pmk.t[:], pmask_d, w=(pmk.r,))
    comp("gpsimd", lambda e: e.memset(zbuf.t[:], 0.0), w=(zbuf.r,))

    comp("scalar", lambda e: e.activation(out=gavs.t[:], in_=gavs.t[:], func=AF.Sigmoid),
         w=(gavs.r,), r=(gavs.r,))
    mm(banks(ps_a, 0, 1),
       [lambda e: e.transpose(ps_a[0:16, 0:1], gavs.t[:], identf.t[0:1, 0:1])],
       r=(gavs.r, identf.r))
    comp("vector", lambda e: e.tensor_copy(out=gTc.t[:], in_=ps_a[0:16, 0:1]),
         w=(gTc.r,), r=(banks(ps_a, 0, 1)[0],))
    mm(banks(ps_a, 512, 528),
       [lambda e: e.matmul(ps_a[:, 512:528], onesfr.t[:], gavs.t[:], start=True, stop=True)],
       r=(onesfr.r, gavs.r))
    comp("vector", lambda e: e.tensor_scalar(out=g1bc.t[:], in0=ps_a[:, 512:528],
                                             scalar1=-1.0, scalar2=1.0,
                                             op0=ALU.mult, op1=ALU.add),
         w=(g1bc.r,), r=(banks(ps_a, 512, 528)[0],))
    comp("vector", lambda e: e.reduce_sum(out=sm4.t[0:4, 0:1], in_=fp[0].t[0:4, :], axis=AX.X),
         w=(sm4.r,), r=(fp[0].r,))
    comp("vector", lambda e: e.tensor_scalar(out=sm4.t[0:4, 1:2], in0=sm4.t[0:4, 0:1],
                                             scalar1=EPS_ATT, scalar2=None, op0=ALU.is_lt),
         w=(sm4.r,), r=(sm4.r,))
    comp("vector", lambda e: e.tensor_scalar(out=sm4.t[0:4, 2:3], in0=sm4.t[0:4, 1:2],
                                             scalar1=-1e30, scalar2=None, op0=ALU.mult),
         w=(sm4.r,), r=(sm4.r,))
    mm(banks(ps_a, 528, 529),
       [lambda e: e.matmul(ps_a[0:1, 528:529], sm4.t[0:4, 1:2], onesf.t[0:4, 0:1],
                           start=True, stop=True)],
       r=(sm4.r, onesf.r))
    comp("vector", lambda e: e.tensor_scalar(out=nta.t[0:1, 0:1], in0=ps_a[0:1, 528:529],
                                             scalar1=3.5, scalar2=None, op0=ALU.is_lt),
         w=(nta.r,), r=(banks(ps_a, 528, 529)[0],))
    mm(banks(ps_a, 529, 530),
       [lambda e: e.matmul(ps_a[0:16, 529:530], onesfr.t[0:1, 0:16], nta.t[0:1, 0:1],
                           start=True, stop=True)],
       r=(onesfr.r, nta.r))
    comp("vector", lambda e: e.tensor_copy(out=nta16.t[:], in_=ps_a[0:16, 529:530]),
         w=(nta16.r,), r=(banks(ps_a, 529, 530)[0],))
    mm(banks(ps_a, 544, 552),
       [lambda e: e.matmul(ps_a[:, 544:552], onesfr.t[:], pmk.t[:], start=True, stop=True)],
       r=(onesfr.r, pmk.r))
    comp("vector", lambda e: e.tensor_copy(out=pmkbc.t[:], in_=ps_a[:, 544:552]),
         w=(pmkbc.r,), r=banks(ps_a, 544, 552))

    for st in range(NT):
        fb = fp[st % 2]
        dma("sync", fb.t[:], x_d[st * 128:(st + 1) * 128, :], w=(fb.r,))
        comp("vector", lambda e, st=st, fb=fb: e.reduce_sum(
            out=s1a.t[:, st:st + 1], in_=fb.t[:], axis=AX.X),
            w=(s1a.r,), r=(fb.r,))
        comp("scalar", lambda e, st=st, fb=fb: e.activation(
            out=fp[2].t[:], in_=fb.t[:], func=AF.Square, accum_out=ssqa.t[:, st:st + 1]),
            w=(fp[2].r, ssqa.r), r=(fb.r,))
    mean_ap = lntmp.t[:, 0:16]
    t1_ap = lntmp.t[:, 16:32]
    comp("vector", lambda e: e.tensor_scalar(out=mean_ap, in0=s1a.t[:], scalar1=1.0 / D,
                                             scalar2=None, op0=ALU.mult),
         w=(lntmp.r,), r=(s1a.r,))
    comp("vector", lambda e: e.tensor_scalar(out=t1_ap, in0=ssqa.t[:], scalar1=1.0 / D,
                                             scalar2=None, op0=ALU.mult),
         w=(lntmp.r,), r=(ssqa.r, lntmp.r))
    comp("vector", lambda e: e.tensor_tensor(out=lntmp.t[:, 32:48], in0=mean_ap,
                                             in1=mean_ap, op=ALU.mult),
         w=(lntmp.r,), r=(lntmp.r,))
    comp("vector", lambda e: e.tensor_tensor(out=t1_ap, in0=t1_ap,
                                             in1=lntmp.t[:, 32:48], op=ALU.subtract),
         w=(lntmp.r,), r=(lntmp.r,))
    comp("vector", lambda e: e.tensor_scalar(out=t1_ap, in0=t1_ap, scalar1=EPS_LN,
                                             scalar2=None, op0=ALU.add),
         w=(lntmp.r,), r=(lntmp.r,))
    comp("scalar", lambda e: e.activation(out=lntmp.t[:, 32:48], in_=t1_ap, func=AF.Sqrt),
         w=(lntmp.r,), r=(lntmp.r,))
    comp("vector", lambda e: e.reciprocal(out=rstda.t[:], in_=lntmp.t[:, 32:48]),
         w=(rstda.r,), r=(lntmp.r,))
    comp("vector", lambda e: e.scalar_tensor_tensor(out=ngmra.t[:], in0=mean_ap,
                                                    scalar=-1.0, in1=rstda.t[:],
                                                    op0=ALU.mult, op1=ALU.mult),
         w=(ngmra.r,), r=(lntmp.r, rstda.r))
    for st in range(NT):
        fb = fp[st % 2]
        xb = bp[st % 2]
        dma("sync", fb.t[:], x_d[st * 128:(st + 1) * 128, :], w=(fb.r,))
        comp("gpsimd", lambda e, st=st, fb=fb, xb=xb: e.tensor_scalar(
            out=xb.t[:], in0=fb.t[:], scalar1=rstda.t[:, st:st + 1],
            scalar2=ngmra.t[:, st:st + 1], op0=ALU.mult, op1=ALU.add),
            w=(xb.r,), r=(fb.r, rstda.r, ngmra.r))
        for half in range(2):
            pv = ps_b[:, half * 512:(half + 1) * 512].bitcast(BF16)
            bres = banks(ps_b, half * 512, (half + 1) * 512)
            steps = []
            for k in range(8):
                dc = half * 8 + k
                steps.append(lambda e, k=k, dc=dc, pv=pv, xb=xb: e.transpose(
                    pv[:, k * 128:(k + 1) * 128], xb.t[:, dc * 128:(dc + 1) * 128],
                    identb.t[:]))
            mm(bres, steps, r=(xb.r, identb.r))
            outap = arenaA.t[:].rearrange("p (dc s) -> p dc s", s=T)[
                :, half * 8:(half + 1) * 8, st * 128:(st + 1) * 128]
            comp("scalar", lambda e, pv=pv, outap=outap: e.activation(
                out=outap, in_=pv.rearrange("p (dc s) -> p dc s", s=128), func=AF.Copy),
                w=(arenaA.r,), r=tuple(bres))

    if DEBUG:
        dma("sync", x1dbg, arenaA.t[:], r=(arenaA.r,))

    a_all = banks(ps_a, 0, 2048)
    b_all = banks(ps_b, 0, 2048)

    def load_wslab(slot, wsrc, h):
        src = wsrc[:, h * 128:(h + 1) * 128].rearrange("(dc p) j -> p dc j", p=128)
        dst = slot.t[:].rearrange("p (dc j) -> p dc j", j=128)
        dma("gpsimd", dst, src, w=(slot.r,))

    sqh, skh, kph, vh, rdb, locb = bp[0], bp[1], bp[2], bp[3], bp[4], bp[5]
    etmp = bp[4]

    for h in range(H):
        slq, slk, slv = wsl[(3 * h) % 4], wsl[(3 * h + 1) % 4], wsl[(3 * h + 2) % 4]
        load_wslab(slq, wqT_d, h)
        load_wslab(slk, wkT_d, h)
        load_wslab(slv, wvT_d, h)

        steps = []
        for dc in range(NDC):
            for nb in range(4):
                steps.append(lambda e, dc=dc, nb=nb, slq=slq: e.matmul(
                    ps_a[:, nb * 512:(nb + 1) * 512], slq.t[:, dc * 128:(dc + 1) * 128],
                    arenaA.t[:, dc * T + nb * 512: dc * T + (nb + 1) * 512],
                    start=(dc == 0), stop=(dc == NDC - 1)))
        mm(a_all, steps, r=(slq.r, arenaA.r))
        if DEBUG and h == 0:
            comp("vector", lambda e: e.tensor_copy(out=fp[1].t[:], in_=ps_a[:, 0:2048]),
                 w=(fp[1].r,), r=a_all)
            dma("sync", qdbg, fp[1].t[:], r=(fp[1].r,))
        comp("vector", lambda e, h=h: e.reduce_sum(out=qsum.t[:, h:h + 1], in_=ps_a[:, 0:2048],
                                                   axis=AX.X),
             w=(qsum.r,), r=a_all)
        comp("scalar", lambda e: e.activation(out=fp[2].t[:], in_=ps_a[:, 0:2048],
                                              func=AF.Relu, scale=-1.0),
             w=(fp[2].r,), r=a_all)
        comp("scalar", lambda e: e.activation(out=etmp.t[:], in_=fp[2].t[:],
                                              func=AF.Exp, scale=-1.0),
             w=(etmp.r,), r=(fp[2].r,))
        comp("vector", lambda e: e.scalar_tensor_tensor(out=sqh.t[:], in0=ps_a[:, 0:2048],
                                                        scalar=0.0, in1=etmp.t[:],
                                                        op0=ALU.max, op1=ALU.add),
             w=(sqh.r,), r=a_all + [etmp.r])
        dma("sync", sq_scr[h], sqh.t[:].rearrange("p s -> p s"), w=(res_sq,), r=(sqh.r,))

        steps = []
        for dc in range(NDC):
            for nb in range(4):
                steps.append(lambda e, dc=dc, nb=nb, slk=slk: e.matmul(
                    ps_b[:, nb * 512:(nb + 1) * 512], slk.t[:, dc * 128:(dc + 1) * 128],
                    arenaA.t[:, dc * T + nb * 512: dc * T + (nb + 1) * 512],
                    start=(dc == 0), stop=(dc == NDC - 1)))
        mm(b_all, steps, r=(slk.r, arenaA.r))
        comp("scalar", lambda e: e.activation(out=fp[2].t[:], in_=ps_b[:, 0:2048],
                                              func=AF.Relu, scale=-1.0),
             w=(fp[2].r,), r=b_all)
        comp("scalar", lambda e: e.activation(out=etmp.t[:], in_=fp[2].t[:],
                                              func=AF.Exp, scale=-1.0),
             w=(etmp.r,), r=(fp[2].r,))
        comp("vector", lambda e: e.scalar_tensor_tensor(out=skh.t[:], in0=ps_b[:, 0:2048],
                                                        scalar=0.0, in1=etmp.t[:],
                                                        op0=ALU.max, op1=ALU.add),
             w=(skh.r,), r=b_all + [etmp.r])

        for g4 in range(4):
            for sti in range(4):
                st = g4 * 4 + sti
                steps = []
                for dc in range(NDC):
                    steps.append(lambda e, dc=dc, st=st, sti=sti, slv=slv: e.matmul(
                        ps_a[:, sti * 512: sti * 512 + 128],
                        arenaA.t[:, dc * T + st * 128: dc * T + (st + 1) * 128],
                        slv.t[:, dc * 128:(dc + 1) * 128],
                        start=(dc == 0), stop=(dc == NDC - 1)))
                mm(banks(ps_a, sti * 512, sti * 512 + 128), steps, r=(slv.r, arenaA.r))
            inap = ps_a[:, 0:2048].rearrange("p (b c) -> p b c", c=512)[:, 0:4, 0:128]
            outap = vh.t[:, g4 * 512:(g4 + 1) * 512].rearrange("p (b c) -> p b c", c=128)
            comp("scalar", lambda e, inap=inap, outap=outap: e.activation(
                out=outap, in_=inap, func=AF.Copy),
                w=(vh.r,), r=a_all)

        for half in range(2):
            pv = ps_b[:, half * 512:(half + 1) * 512].bitcast(BF16)
            bres = banks(ps_b, half * 512, (half + 1) * 512)
            steps = []
            for k in range(8):
                st = half * 8 + k
                steps.append(lambda e, k=k, st=st, pv=pv: e.transpose(
                    pv[:, k * 128:(k + 1) * 128], skh.t[:, st * 128:(st + 1) * 128],
                    identb.t[:]))
            mm(bres, steps, r=(skh.r, identb.r))
            comp("scalar", lambda e, pv=pv, half=half: e.activation(
                out=kph.t[:, half * 1024:(half + 1) * 1024], in_=pv, func=AF.Copy),
                w=(kph.r,), r=tuple(bres))

        st_ps = ps_a[:, 0:132]
        st_bk = banks(ps_a, 0, 132)
        sc_ps = ps_a[:, 1024:1408]
        sc_bk = banks(ps_a, 1024, 1408)
        den_ps = ps_a[0:1, 512:768]
        den_bk = banks(ps_a, 512, 768)
        num_bk = b_all

        for sc in range(NSC):
            st0, st1 = 2 * sc, 2 * sc + 1
            stb = stsb[sc % 2]
            scbb = scb[sc % 2]
            if sc > 0:
                comp("scalar", lambda e, stb=stb: e.activation(
                    out=stb.t[:], in_=st_ps, func=AF.Copy),
                    w=(stb.r,), r=st_bk)
            steps = []
            for cb in (st0, st1):
                first = (sc == 0 and cb == st0)
                steps.append(lambda e, cb=cb, first=first: e.matmul(
                    st_ps[:, 0:128], kph.t[:, cb * 128:(cb + 1) * 128],
                    vh.t[:, cb * 128:(cb + 1) * 128], start=first,
                    stop=(sc == NSC - 1 and cb == st1)))
                steps.append(lambda e, cb=cb: e.matmul(
                    st_ps[:, 128:129], kph.t[:, cb * 128:(cb + 1) * 128],
                    onesb.t[:, 0:1], start=False, stop=(sc == NSC - 1 and cb == st1)))
            stread = [stb.r] if sc > 0 else []
            mm(st_bk, steps, r=[kph.r, vh.r, onesb.r] + stread)
            pairs = [(st0, st0), (st0, st1), (st1, st1)]
            steps = []
            for pi, (jb, ib) in enumerate(pairs):
                steps.append(lambda e, pi=pi, jb=jb, ib=ib: e.matmul(
                    sc_ps[:, pi * 128:(pi + 1) * 128], skh.t[:, jb * 128:(jb + 1) * 128],
                    sqh.t[:, ib * 128:(ib + 1) * 128], start=True, stop=True))
            mm(sc_bk, steps, r=(skh.r, sqh.r))
            comp("vector", lambda e, scbb=scbb: e.tensor_tensor(
                out=scbb.t[:], in0=sc_ps, in1=mask3.t[:], op=ALU.mult),
                w=(scbb.r,), r=sc_bk + [mask3.r])
            specs = []
            for pi, (jb, ib) in enumerate(pairs):
                specs.append(("ssum", pi, (ib - st0) * 128))
            if sc > 0:
                for cb in (st0, st1):
                    specs.append(("zden", cb, (cb - st0) * 128))
            steps = []
            seen = set()
            for i, (kind, aa, col) in enumerate(specs):
                stp = (i == len(specs) - 1)
                first = col not in seen and not seen
                seen.add(col)
                strt = (i == 0)
                if kind == "ssum":
                    steps.append(lambda e, pi=aa, col=col, stp=stp, strt=strt, scbb=scbb: e.matmul(
                        den_ps[:, col:col + 128], onesb.t[:, 0:1],
                        scbb.t[:, pi * 128:(pi + 1) * 128], start=strt, stop=stp))
                else:
                    steps.append(lambda e, cb=aa, col=col, stp=stp, stb=stb: e.matmul(
                        den_ps[:, col:col + 128], stb.t[:, 128:129],
                        sqh.t[:, cb * 128:(cb + 1) * 128], start=False, stop=stp))
            mm(den_bk, steps, r=[onesb.r, scbb.r, sqh.r] + stread)
            comp("vector", lambda e, sc=sc: e.tensor_copy(
                out=fp[0].t[0:1, sc * 256:(sc + 1) * 256], in_=den_ps[:, 0:256]),
                w=(fp[0].r,), r=den_bk)
            for ib in (st0, st1):
                reg = ps_b[:, ib * 128:(ib + 1) * 128]
                regb = banks(ps_b, ib * 128, (ib + 1) * 128)
                jbs = [jb for jb in (st0, st1) if jb <= ib]
                nsteps = len(jbs) + (1 if sc > 0 else 0)
                steps = []
                for k, jb in enumerate(jbs):
                    pi = pairs.index((jb, ib))
                    steps.append(lambda e, reg=reg, jb=jb, pi=pi, k=k, scbb=scbb,
                                 stp=(k == nsteps - 1): e.matmul(
                        reg, vh.t[:, jb * 128:(jb + 1) * 128],
                        scbb.t[:, pi * 128:(pi + 1) * 128],
                        start=(k == 0), stop=stp))
                if sc > 0:
                    steps.append(lambda e, reg=reg, ib=ib, stb=stb: e.matmul(
                        reg, stb.t[:, 0:128], sqh.t[:, ib * 128:(ib + 1) * 128],
                        start=False, stop=True))
                mm(regb, steps, r=[vh.r, scbb.r, sqh.r] + stread)
        comp("vector", lambda e: e.tensor_copy(out=locb.t[:], in_=ps_b[:, 0:2048]),
             w=(locb.r,), r=num_bk)
        dma("sync", at_scr[h], locb.t[:], w=(res_at,), r=(locb.r,))
        dma("sync", den_scr[h], fp[0].t[0:1, :], w=(res_den,), r=(fp[0].r,))
        tot = stsb[0]
        comp("scalar", lambda e, tot=tot: e.activation(out=tot.t[:], in_=st_ps,
                                                       func=AF.Copy),
             w=(tot.r,), r=st_bk)
        dma("sync", st_scr[h * 128:(h + 1) * 128, :], tot.t[:], w=(res_st,), r=(tot.r,))

    comp("gpsimd", lambda e: e.collective_compute(
        "AllGather", ALU.bypass, replica_groups=[list(range(NCORE))],
        ins=[st_scr.ap().opt()], outs=[stg_all.ap().opt()]),
        w=(res_stg,), r=(res_st,), inc=1, incsem="cc")

    steps = []
    for h in range(H):
        steps.append(lambda e, h=h: e.matmul(
            ps_a[0:4, h:h + 1], mnTf.t[:, h * 4:(h + 1) * 4], qsum.t[:, h:h + 1],
            start=True, stop=True))
    mm(banks(ps_a, 0, 16), steps, r=(mnTf.r, qsum.r))
    comp("vector", lambda e: e.tensor_scalar(out=zbuf.t[0:4, :], in0=ps_a[0:4, 0:16],
                                             scalar1=RELSC, scalar2=None, op0=ALU.mult),
         w=(zbuf.r,), r=banks(ps_a, 0, 16))
    if DEBUG:
        dma("sync", zbdbg, zbuf.t[:], r=(zbuf.r,))
    dma("gpsimd", cc_in.ap(), zbuf.t[:], w=(res_ccin,), r=(zbuf.r,))
    comp("gpsimd", lambda e: e.collective_compute(
        "AllReduce", ALU.add, replica_groups=[list(range(NCORE))],
        ins=[cc_in.ap().opt()], outs=[cc_out.ap().opt()]),
        w=(res_ccout,), r=(res_ccin,), inc=1, incsem="cc")
    dma("gpsimd", relg.t[:], cc_out.ap(), w=(relg.r,), r=(res_ccout,),
        xw=(("cc", 2),))
    comp("vector", lambda e: e.tensor_scalar(out=relg.t[0:4, :], in0=relg.t[0:4, :],
                                             scalar1=sm4.t[0:4, 2:3], scalar2=None,
                                             op0=ALU.add),
         w=(relg.r,), r=(relg.r, sm4.r))
    mm(banks(ps_a, 16, 32),
       [lambda e: e.transpose(ps_a[0:16, 16:20], relg.t[0:4, 0:16], identf.t[0:4, 0:4])],
       r=(relg.r, identf.r))
    comp("vector", lambda e: e.tensor_copy(out=relT.t[0:16, 0:4], in_=ps_a[0:16, 16:20]),
         w=(relT.r,), r=banks(ps_a, 16, 32))
    comp("vector", lambda e: e.reduce_max(out=m1c.t[:, 0:1], in_=relT.t[0:16, 0:4],
                                          axis=AX.X),
         w=(m1c.r,), r=(relT.r,))
    comp("vector", lambda e: e.tensor_scalar(out=m1c.t[:, 1:2], in0=m1c.t[:, 0:1],
                                             scalar1=-1.0, scalar2=None, op0=ALU.mult),
         w=(m1c.r,), r=(m1c.r,))
    comp("scalar", lambda e: e.activation(out=relT.t[0:16, 4:8], in_=relT.t[0:16, 0:4],
                                          func=AF.Exp, bias=m1c.t[:, 1:2]),
         w=(relT.r,), r=(relT.r, m1c.r))
    comp("vector", lambda e: e.reduce_sum(out=m1c.t[:, 2:3], in_=relT.t[0:16, 4:8],
                                          axis=AX.X),
         w=(m1c.r,), r=(relT.r,))
    comp("vector", lambda e: e.reciprocal(out=m1c.t[:, 3:4], in_=m1c.t[:, 2:3]),
         w=(m1c.r,), r=(m1c.r,))
    comp("vector", lambda e: e.tensor_scalar(out=wgT.t[:], in0=relT.t[0:16, 4:8],
                                             scalar1=m1c.t[:, 3:4], scalar2=nta16.t[:],
                                             op0=ALU.mult, op1=ALU.mult),
         w=(wgT.r,), r=(relT.r, m1c.r, nta16.r))
    comp("vector", lambda e: e.tensor_scalar(out=wgT.t[:], in0=wgT.t[:],
                                             scalar1=gTc.t[:], scalar2=None, op0=ALU.mult),
         w=(wgT.r,), r=(wgT.r, gTc.r))
    mm(banks(ps_a, 32, 48),
       [lambda e: e.transpose(ps_a[0:4, 32:48], wgT.t[:], identf.t[0:16, 0:16])],
       r=(wgT.r, identf.r))
    comp("vector", lambda e: e.tensor_copy(out=wgmh.t[:], in_=ps_a[0:4, 32:48]),
         w=(wgmh.r,), r=banks(ps_a, 32, 48))

    if DEBUG:
        comp("vector", lambda e: e.tensor_copy(out=lg.t[:, 0:16], in_=qsum.t[:]),
             w=(lg.r,), r=(qsum.r,))
        comp("vector", lambda e: e.tensor_copy(out=lg.t[:, 16:32], in_=relg.t[:]),
             r=(relg.r,))
        comp("vector", lambda e: e.tensor_copy(out=r1.t[0:4, 0:8], in_=wgmh.t[:, 0:8]),
             w=(r1.r,), r=(wgmh.r,))
        dma("sync", smdbg[:, 0:16], lg.t[:, 0:16], r=(lg.r,))
        dma("sync", smdbg[:, 16:32], lg.t[:, 16:32], r=(lg.r,))
        dma("sync", smdbg[0:4, 32:40], r1.t[0:4, 0:8], r=(r1.r,))
        dma("sync", smdbg[:, 48:64], g1bc.t[:], r=(g1bc.r,))
        dma("sync", smdbg[0:4, 40:48], sm4.t[:], r=(sm4.r,))
        dma("sync", smdbg[0:1, 32:34], nta.t[:], r=(nta.r,))

    sqh2, locb2, rmb, sqm, t1b4 = bp[0], bp[1], bp[2], bp[3], bp[4]
    for h in range(H):
        msb = msl[h % 2]
        src = memr_d[h].rearrange("m p j -> p m j")
        dma("gpsimd", msb.t[:].rearrange("p (m j) -> p m j", j=128), src, w=(msb.r,))
        dma("sync", sqh2.t[:], sq_scr[h], w=(sqh2.r,), r=(res_sq,))
        dma("sync", locb2.t[:], at_scr[h], w=(locb2.r,), r=(res_at,))
        dma("sync", fp[2].t[0:1, :], den_scr[h], w=(fp[2].r,), r=(res_den,))
        gsrc = stg_all.ap().rearrange("(c r) f -> c r f", r=H * HD)[
            :, h * 128:(h + 1) * 128, :].rearrange("c p f -> p c f")
        dma("sync", bp[5].t[:, 0:NCORE * 132].rearrange("p (c f) -> p c f", f=132),
            gsrc, w=(bp[5].r,), r=(res_stg,), xw=(("cc", 1),))
        comp("vector", lambda e: e.tensor_tensor(
            out=bp[5].t[:, 0:NCORE * 132].rearrange("p (c f) -> p c f", f=132),
            in0=bp[5].t[:, 0:NCORE * 132].rearrange("p (c f) -> p c f", f=132),
            in1=pmkbc.t[:].unsqueeze(2).broadcast_to([128, NCORE, 132]),
            op=ALU.mult),
            w=(bp[5].r,), r=(bp[5].r, pmkbc.r))
        comp("vector", lambda e: e.tensor_reduce(
            out=bp[5].t[:, 1056:1188],
            in_=bp[5].t[:, 0:NCORE * 132].rearrange("p (c f) -> p f c", c=NCORE),
            axis=AX.X, op=ALU.add),
            w=(bp[5].r,), r=(bp[5].r,))
        sg_kv = bp[5].t[:, 1056:1184]
        sg_z = bp[5].t[:, 1184:1185]
        steps = []
        for st in range(NT):
            steps.append(lambda e, st=st, h=h: e.matmul(
                ps_a[:, 1536 + st * 4:1536 + (st + 1) * 4],
                sqh2.t[:, st * 128:(st + 1) * 128], mnTb.t[:, h * 4:(h + 1) * 4],
                start=True, stop=True))
            steps.append(lambda e, st=st: e.matmul(
                ps_a[:, 1600 + st:1601 + st],
                sqh2.t[:, st * 128:(st + 1) * 128], bp[5].t[:, 1184:1185],
                start=True, stop=True))
            steps.append(lambda e, st=st: e.transpose(
                ps_a[:, 1616 + st:1617 + st].bitcast(F32),
                fp[2].t[0:1, st * 128:(st + 1) * 128], identf.t[0:1, 0:1]))
        mm(banks(ps_a, 1536, 2048), steps,
           r=(sqh2.r, mnTb.r, bp[5].r, fp[2].r, identf.r))
        comp("vector", lambda e: e.tensor_copy(out=fp[1].t[:, 16:32],
                                               in_=ps_a[:, 1616:1632]),
             w=(fp[1].r,), r=banks(ps_a, 1536, 2048))
        comp("vector", lambda e: e.scalar_tensor_tensor(
            out=fp[1].t[:, 0:16], in0=ps_a[:, 1600:1616], scalar=EPS_ATT,
            in1=fp[1].t[:, 16:32], op0=ALU.add, op1=ALU.add),
            w=(fp[1].r,), r=banks(ps_a, 1536, 2048) + [fp[1].r])
        comp("vector", lambda e: e.reciprocal(out=fp[1].t[:, 0:16],
                                              in_=fp[1].t[:, 0:16]),
             w=(fp[1].r,), r=(fp[1].r,))
        comp("vector", lambda e: e.tensor_scalar(out=fp[0].t[:, 0:64],
                                                 in0=ps_a[:, 1536:1600],
                                                 scalar1=EPS_ATT, scalar2=None,
                                                 op0=ALU.add),
             w=(fp[0].r,), r=banks(ps_a, 1536, 2048))
        comp("vector", lambda e: e.reciprocal(out=fp[0].t[:, 0:64],
                                              in_=fp[0].t[:, 0:64]),
             w=(fp[0].r,), r=(fp[0].r,))
        steps = []
        for st in range(NT):
            steps.append(lambda e, st=st: e.transpose(
                ps_a[0:4, st * 128:(st + 1) * 128].bitcast(F32),
                fp[0].t[:, st * 4:(st + 1) * 4], identf.t[:]))
        mm(a_all, steps, r=(fp[0].r, identf.r))
        comp("vector", lambda e, h=h: e.tensor_scalar(
            out=rmb.t[0:4, :], in0=ps_a[0:4, 0:2048], scalar1=wgmh.t[:, h:h + 1],
            scalar2=None, op0=ALU.mult),
            w=(rmb.r,), r=a_all + [wgmh.r])
        steps = []
        for st in range(NT):
            steps.append(lambda e, st=st: e.transpose(
                ps_a[0:1, st * 128:(st + 1) * 128].bitcast(F32),
                fp[1].t[:, st:st + 1], identf.t[:]))
        mm(a_all, steps, r=(fp[1].r, identf.r))
        comp("scalar", lambda e: e.copy(out=fp[1].t[0:1, :], in_=ps_a[0:1, 0:2048]),
             w=(fp[1].r,), r=a_all)
        for half in range(2):
            bcb = banks(ps_a, 0, 1024)
            steps = []
            for nb in range(2):
                steps.append(lambda e, half=half, nb=nb: e.matmul(
                    ps_a[:, nb * 512:(nb + 1) * 512], onesfr.t[:],
                    fp[1].t[0:1, half * 1024 + nb * 512: half * 1024 + (nb + 1) * 512],
                    start=True, stop=True))
            mm(bcb, steps, r=(onesfr.r, fp[1].r))
            comp("vector", lambda e, half=half: e.tensor_copy(
                out=sqm.t[:, half * 1024:(half + 1) * 1024], in_=ps_a[:, 0:1024]),
                w=(sqm.r,), r=bcb)
        steps = []
        for nb in range(4):
            steps.append(lambda e, nb=nb, sg_kv=sg_kv: e.matmul(
                ps_b[:, nb * 512:(nb + 1) * 512], sg_kv,
                sqh2.t[:, nb * 512:(nb + 1) * 512], start=True, stop=True))
        mm(b_all, steps, r=(bp[5].r, sqh2.r))
        comp("vector", lambda e: e.tensor_tensor(out=t1b4.t[:], in0=ps_b[:, 0:2048],
                                                 in1=locb2.t[:], op=ALU.add),
             w=(t1b4.r,), r=b_all + [locb2.r])
        comp("vector", lambda e, h=h: e.scalar_tensor_tensor(
            out=locb2.t[:], in0=t1b4.t[:], scalar=g1bc.t[:, h:h + 1],
            in1=sqm.t[:], op0=ALU.mult, op1=ALU.mult),
            w=(locb2.r,), r=(t1b4.r, g1bc.r, sqm.r))
        for m in range(M):
            bcb = a_all
            steps = []
            for nb in range(4):
                steps.append(lambda e, m=m, nb=nb: e.matmul(
                    ps_a[:, nb * 512:(nb + 1) * 512], selmat.t[0:4, m * 128:(m + 1) * 128],
                    rmb.t[0:4, nb * 512:(nb + 1) * 512], start=True, stop=True))
            mm(bcb, steps, r=(selmat.r, rmb.r))
            comp("vector", lambda e: e.tensor_tensor(out=sqm.t[:], in0=sqh2.t[:],
                                                     in1=ps_a[:, 0:2048], op=ALU.mult),
                 w=(sqm.r,), r=[sqh2.r] + a_all)
            steps = []
            for nb in range(4):
                steps.append(lambda e, m=m, nb=nb, msb=msb: e.matmul(
                    ps_b[:, nb * 512:(nb + 1) * 512], msb.t[:, m * 128:(m + 1) * 128],
                    sqm.t[:, nb * 512:(nb + 1) * 512], start=(m == 0), stop=(m == M - 1)))
            mm(b_all, steps, r=(msb.r, sqm.r))
        comp("vector", lambda e: e.tensor_tensor(out=t1b4.t[:], in0=ps_b[:, 0:2048],
                                                 in1=locb2.t[:], op=ALU.add),
             w=(t1b4.r,), r=b_all + [locb2.r])
        dma("sync", at_scr[h], t1b4.t[:], w=(res_at,), r=(t1b4.r,))

    for dc in range(NDC):
        src = woT_d[dc * 128:(dc + 1) * 128, :]
        dma("gpsimd", arenaB.t[:, dc * T:(dc + 1) * T], src, w=(arenaB.r,))

    for st in range(NT):
        at = bp[st % 2]
        fb = fp[st % 2]
        src = at_scr[:, :, st * 128:(st + 1) * 128].rearrange("h p j -> p h j")
        dma("sync", at.t[:].rearrange("p (h j) -> p h j", j=128), src, w=(at.r,), r=(res_at,))
        dma("sync", fb.t[:], x_d[st * 128:(st + 1) * 128, :], w=(fb.r,))
        pst = ps_a if st % 2 == 0 else ps_b
        pbk = a_all if st % 2 == 0 else b_all
        steps = []
        for dc in range(NDC):
            for nb in range(4):
                steps.append(lambda e, dc=dc, nb=nb, at=at, pst=pst: e.matmul(
                    pst[:, nb * 512:(nb + 1) * 512], at.t[:, dc * 128:(dc + 1) * 128],
                    arenaB.t[:, dc * T + nb * 512: dc * T + (nb + 1) * 512],
                    start=(dc == 0), stop=(dc == NDC - 1)))
        mm(pbk, steps, r=(at.r, arenaB.r))
        comp("vector", lambda e, fb=fb, pst=pst: e.tensor_tensor(
            out=fp[2].t[:], in0=pst[:, 0:2048], in1=fb.t[:], op=ALU.add),
            w=(fp[2].r,), r=pbk + [fb.r])
        comp("vector", lambda e, st=st: e.reduce_sum(out=s1a.t[:, st:st + 1],
                                                     in_=fp[2].t[:], axis=AX.X),
             w=(s1a.r,), r=(fp[2].r,))
        comp("scalar", lambda e, st=st, fb=fb: e.activation(
            out=fb.t[:], in_=fp[2].t[:], func=AF.Square, accum_out=ssqa.t[:, st:st + 1]),
            w=(fb.r, ssqa.r), r=(fp[2].r,))
        dma("sync", xp_scr[st], fp[2].t[:], w=(res_xp,), r=(fp[2].r,))

    comp("vector", lambda e: e.tensor_scalar(out=mean_ap, in0=s1a.t[:], scalar1=1.0 / D,
                                             scalar2=None, op0=ALU.mult),
         w=(lntmp.r,), r=(s1a.r,))
    comp("vector", lambda e: e.tensor_scalar(out=t1_ap, in0=ssqa.t[:], scalar1=1.0 / D,
                                             scalar2=None, op0=ALU.mult),
         w=(lntmp.r,), r=(ssqa.r, lntmp.r))
    comp("vector", lambda e: e.tensor_tensor(out=lntmp.t[:, 32:48], in0=mean_ap,
                                             in1=mean_ap, op=ALU.mult),
         w=(lntmp.r,), r=(lntmp.r,))
    comp("vector", lambda e: e.tensor_tensor(out=t1_ap, in0=t1_ap,
                                             in1=lntmp.t[:, 32:48], op=ALU.subtract),
         w=(lntmp.r,), r=(lntmp.r,))
    comp("vector", lambda e: e.tensor_scalar(out=t1_ap, in0=t1_ap, scalar1=EPS_LN,
                                             scalar2=None, op0=ALU.add),
         w=(lntmp.r,), r=(lntmp.r,))
    comp("scalar", lambda e: e.activation(out=lntmp.t[:, 32:48], in_=t1_ap, func=AF.Sqrt),
         w=(lntmp.r,), r=(lntmp.r,))
    comp("vector", lambda e: e.reciprocal(out=rstda.t[:], in_=lntmp.t[:, 32:48]),
         w=(rstda.r,), r=(lntmp.r,))
    comp("vector", lambda e: e.scalar_tensor_tensor(out=ngmra.t[:], in0=mean_ap,
                                                    scalar=-1.0, in1=rstda.t[:],
                                                    op0=ALU.mult, op1=ALU.mult),
         w=(ngmra.r,), r=(lntmp.r, rstda.r))

    gaA = wsl[0]
    dma("gpsimd", gaG.t[:].rearrange("p (dc j) -> p dc j", j=8),
        G_d.rearrange("(dc p) j -> p dc j", p=128), w=(gaG.r,))
    dma("gpsimd", gaA.t[:].rearrange("p (dc j) -> p dc j", j=128),
        laT_d.rearrange("(dc p) j -> p dc j", p=128), w=(gaA.r,))

    for st in range(NT):
        fb = fp[st % 2]
        x2 = bp[2 + st % 2]
        dma("sync", fb.t[:], xp_scr[st], w=(fb.r,), r=(res_xp,))
        comp("gpsimd", lambda e, st=st, fb=fb, x2=x2: e.tensor_scalar(
            out=x2.t[:], in0=fb.t[:], scalar1=rstda.t[:, st:st + 1],
            scalar2=ngmra.t[:, st:st + 1], op0=ALU.mult, op1=ALU.add),
            w=(x2.r,), r=(fb.r, rstda.r, ngmra.r))
        for half in range(2):
            pv = ps_a[:, half * 512:(half + 1) * 512].bitcast(BF16)
            bres = banks(ps_a, half * 512, (half + 1) * 512)
            steps = []
            for k in range(8):
                dc = half * 8 + k
                steps.append(lambda e, k=k, dc=dc, pv=pv, x2=x2: e.transpose(
                    pv[:, k * 128:(k + 1) * 128], x2.t[:, dc * 128:(dc + 1) * 128],
                    identb.t[:]))
            mm(bres, steps, r=(x2.r, identb.r))
            outap = arenaA.t[:].rearrange("p (dc s) -> p dc s", s=T)[
                :, half * 8:(half + 1) * 8, st * 128:(st + 1) * 128]
            comp("scalar", lambda e, pv=pv, outap=outap: e.activation(
                out=outap, in_=pv.rearrange("p (dc j) -> p dc j", j=128), func=AF.Copy),
                w=(arenaA.r,), r=tuple(bres))
        rp = ps_b[:, 0:136]
        rbk = banks(ps_b, 0, 136)
        steps = []
        for dc in range(NDC):
            steps.append(lambda e, dc=dc, st=st: e.matmul(
                rp[:, 0:8], arenaA.t[:, dc * T + st * 128: dc * T + (st + 1) * 128],
                gaG.t[:, dc * 8:(dc + 1) * 8], start=(dc == 0), stop=(dc == NDC - 1)))
            steps.append(lambda e, dc=dc, st=st: e.matmul(
                rp[:, 8:136], arenaA.t[:, dc * T + st * 128: dc * T + (st + 1) * 128],
                gaA.t[:, dc * 128:(dc + 1) * 128], start=False, stop=(dc == NDC - 1)))
        mm(rbk, steps, r=(arenaA.r, gaG.r, gaA.r))
        comp("vector", lambda e: e.tensor_copy(out=lg.t[:, 0:8], in_=rp[:, 0:8]),
             w=(lg.r,), r=rbk)
        comp("vector", lambda e: e.reduce_max(out=r1.t[:, 0:1], in_=lg.t[:, 0:8], axis=AX.X),
             w=(r1.r,), r=(lg.r,))
        comp("vector", lambda e: e.tensor_scalar(out=r1.t[:, 1:2], in0=r1.t[:, 0:1],
                                                 scalar1=-1.0, scalar2=None, op0=ALU.mult),
             w=(r1.r,), r=(r1.r,))
        comp("scalar", lambda e: e.activation(out=lg.t[:, 8:16], in_=lg.t[:, 0:8],
                                              func=AF.Exp, bias=r1.t[:, 1:2]),
             w=(lg.r,), r=(lg.r, r1.r))
        comp("vector", lambda e: e.reduce_sum(out=r1.t[:, 2:3], in_=lg.t[:, 8:16], axis=AX.X),
             w=(r1.r,), r=(lg.r,))
        comp("vector", lambda e: e.reciprocal(out=r1.t[:, 3:4], in_=r1.t[:, 2:3]),
             w=(r1.r,), r=(r1.r,))
        comp("vector", lambda e: e.tensor_scalar(out=lg.t[:, 16:24], in0=lg.t[:, 8:16],
                                                 scalar1=r1.t[:, 3:4], scalar2=None,
                                                 op0=ALU.mult),
             w=(lg.r,), r=(lg.r, r1.r))
        comp("vector", lambda e: e.reduce_max(out=r1.t[:, 4:5], in_=lg.t[:, 16:24], axis=AX.X),
             w=(r1.r,), r=(lg.r,))
        comp("vector", lambda e: e.tensor_scalar(out=lg.t[:, 24:32], in0=lg.t[:, 16:24],
                                                 scalar1=r1.t[:, 4:5], scalar2=None,
                                                 op0=ALU.is_lt),
             w=(lg.r,), r=(lg.r, r1.r))
        comp("vector", lambda e: e.tensor_tensor(out=lg.t[:, 8:16], in0=lg.t[:, 16:24],
                                                 in1=lg.t[:, 24:32], op=ALU.mult),
             w=(lg.r,), r=(lg.r,))
        comp("vector", lambda e: e.reduce_max(out=r1.t[:, 5:6], in_=lg.t[:, 8:16], axis=AX.X),
             w=(r1.r,), r=(lg.r,))
        comp("vector", lambda e: e.tensor_scalar(out=lg.t[:, 24:32], in0=lg.t[:, 16:24],
                                                 scalar1=r1.t[:, 5:6], scalar2=None,
                                                 op0=ALU.is_ge),
             w=(lg.r,), r=(lg.r, r1.r))
        comp("vector", lambda e, st=st: e.tensor_tensor(
            out=wall.t[:, st * 8:(st + 1) * 8], in0=lg.t[:, 16:24], in1=lg.t[:, 24:32],
            op=ALU.mult),
            w=(wall.r,), r=(lg.r,))
        if DEBUG and st == 0:
            dma("sync", rtdbg[:, 0:32], lg.t[:], r=(lg.r,))
            dma("sync", rtdbg[:, 32:40], r1.t[:], r=(r1.r,))
        comp("vector", lambda e, st=st: e.tensor_copy(
            out=wexp.t[:].rearrange("p (e r) -> p e r", r=16),
            in_=wall.t[:, st * 8:(st + 1) * 8].unsqueeze(2).broadcast_to([128, 8, 16])),
            w=(wexp.r,), r=(wall.r,))
        lrs = bp[4]
        comp("vector", lambda e: e.tensor_tensor(out=lrs.t[:, 0:128], in0=rp[:, 8:136],
                                                 in1=wexp.t[:], op=ALU.mult),
             w=(lrs.r,), r=rbk + [wexp.r])
        pv = ps_b[:, 512:576].bitcast(BF16)
        bres = banks(ps_b, 512, 576)
        mm(bres, [lambda e, pv=pv, lrs=lrs: e.transpose(pv[:, 0:128], lrs.t[:, 0:128],
                                                        identb.t[:])],
           r=(lrs.r, identb.r))
        comp("scalar", lambda e, st=st, pv=pv: e.activation(
            out=lrw.t[:, st * 128:(st + 1) * 128], in_=pv[:, 0:128], func=AF.Copy),
            w=(lrw.r,), r=tuple(bres))
        pw = ps_b[0:8, 640:768]
        bres2 = banks(ps_b, 640, 768)
        mm(bres2, [lambda e, st=st, pw=pw: e.transpose(
            pw, wall.t[:, st * 8:(st + 1) * 8], identf.t[:])],
           r=(wall.r, identf.r))
        comp("scalar", lambda e, st=st, pw=pw: e.activation(
            out=wT8.t[0:8, st * 128:(st + 1) * 128], in_=pw, func=AF.Copy),
            w=(wT8.r,), r=tuple(bres2))

    if DEBUG:
        dma("sync", x2dbg, arenaA.t[:], r=(arenaA.r,))
        dma("sync", wldbg, wall.t[:], r=(wall.r,))

    t1b, t2b = bp[0], bp[1]
    for fc in range(NDC):
        slg, slu = wsl[(2 * fc) % 4], wsl[(2 * fc + 1) % 4]
        load_wslab(slg, gupg_d, fc)
        load_wslab(slu, gupu_d, fc)
        e_x = fc // 2
        for sh in range(2):
            par = (fc * 2 + sh) % 2
            pst = ps_a if par == 0 else ps_b
            pbk4 = a_all if par == 0 else b_all
            opst = ps_b if par == 0 else ps_a
            opbk = banks(ps_b, 0, 512) if par == 0 else banks(ps_a, 0, 512)
            if sh == 0 and fc % 2 == 0:
                for nb in range(4):
                    mm(opbk, [lambda e, e_x=e_x, nb=nb, opst=opst: e.matmul(
                        opst[:, 0:512], selmat.t[:, e_x * 128:(e_x + 1) * 128],
                        wT8.t[0:8, nb * 512:(nb + 1) * 512], start=True, stop=True)],
                       r=(selmat.r, wT8.r))
                    comp("vector", lambda e, nb=nb, opst=opst: e.tensor_copy(
                        out=wbc.t[:, nb * 512:(nb + 1) * 512], in_=opst[:, 0:512]),
                        w=(wbc.r,), r=opbk)
            steps = []
            for dc in range(NDC):
                for nb in range(2):
                    steps.append(lambda e, dc=dc, nb=nb, sh=sh, slg=slg, pst=pst: e.matmul(
                        pst[:, nb * 512:(nb + 1) * 512], slg.t[:, dc * 128:(dc + 1) * 128],
                        arenaA.t[:, dc * T + sh * 1024 + nb * 512:
                                 dc * T + sh * 1024 + (nb + 1) * 512],
                        start=(dc == 0), stop=(dc == NDC - 1)))
            mm(pbk4[0:2], steps, r=(slg.r, arenaA.r))
            steps = []
            for dc in range(NDC):
                for nb in range(2):
                    steps.append(lambda e, dc=dc, nb=nb, sh=sh, slu=slu, pst=pst: e.matmul(
                        pst[:, 1024 + nb * 512:1024 + (nb + 1) * 512],
                        slu.t[:, dc * 128:(dc + 1) * 128],
                        arenaA.t[:, dc * T + sh * 1024 + nb * 512:
                                 dc * T + sh * 1024 + (nb + 1) * 512],
                        start=(dc == 0), stop=(dc == NDC - 1)))
            mm(pbk4[2:4], steps, r=(slu.r, arenaA.r))
            comp("scalar", lambda e, pst=pst: e.activation(out=t1b.t[:, 0:1024],
                                                           in_=pst[:, 0:1024], func=AF.Silu),
                 w=(t1b.r,), r=pbk4[0:2])
            comp("vector", lambda e, pst=pst: e.tensor_tensor(
                out=t2b.t[:, 0:1024], in0=t1b.t[:, 0:1024], in1=pst[:, 1024:2048],
                op=ALU.mult),
                w=(t2b.r,), r=[t1b.r] + pbk4[2:4])
            comp("vector", lambda e, fc=fc, sh=sh: e.tensor_tensor(
                out=arenaB.t[:, fc * T + sh * 1024: fc * T + (sh + 1) * 1024],
                in0=t2b.t[:, 0:1024], in1=wbc.t[:, sh * 1024:(sh + 1) * 1024],
                op=ALU.mult),
                w=(arenaB.r,), r=(t2b.r, wbc.r))

    if DEBUG:
        dma("sync", gadbg, arenaB.t[:], r=(arenaB.r,))

    for dc in range(NDC):
        dma("gpsimd", arenaA.t[:, dc * T:(dc + 1) * T],
            downT_d[dc * 128:(dc + 1) * 128, :], w=(arenaA.r,))
    lbTs = wsl[3]
    dma("gpsimd", lbTs.t[:], lbT_d, w=(lbTs.r,))

    for st in range(NT):
        fb = fp[st % 2]
        dma("sync", fb.t[:], xp_scr[st], w=(fb.r,), r=(res_xp,))
        pst = ps_a if st % 2 == 0 else ps_b
        pbk = a_all if st % 2 == 0 else b_all
        steps = []
        for fc in range(NDC + 1):
            for nb in range(4):
                if fc < NDC:
                    steps.append(lambda e, fc=fc, nb=nb, st=st, pst=pst: e.matmul(
                        pst[:, nb * 512:(nb + 1) * 512],
                        arenaB.t[:, fc * T + st * 128: fc * T + (st + 1) * 128],
                        arenaA.t[:, fc * T + nb * 512: fc * T + (nb + 1) * 512],
                        start=(fc == 0), stop=False))
                else:
                    steps.append(lambda e, nb=nb, st=st, pst=pst: e.matmul(
                        pst[:, nb * 512:(nb + 1) * 512],
                        lrw.t[:, st * 128:(st + 1) * 128],
                        lbTs.t[:, nb * 512:(nb + 1) * 512],
                        start=False, stop=True))
        mm(pbk, steps, r=(arenaA.r, arenaB.r, lrw.r, lbTs.r))
        comp("vector", lambda e, fb=fb, pst=pst: e.tensor_tensor(
            out=fp[2].t[:], in0=pst[:, 0:2048], in1=fb.t[:], op=ALU.add),
            w=(fp[2].r,), r=pbk + [fb.r])
        dma("sync", y_d[st * 128:(st + 1) * 128, :], fp[2].t[:], r=(fp[2].r,))

    final_sy = p.cnt["sy"]

    with nc.allow_low_precision(reason="bf16 matmul operands / bf16 psum transposes are intentional"), nc.Block() as block:
        for eng in ENGS:
            def make(eng=eng):
                def body(e):
                    for ws, fn, sem, inc in p.ops[eng]:
                        for s, v in ws:
                            e.wait_ge(sems[s], v)
                        ins = fn(e)
                        if sem is not None:
                            ins.then_inc(sems[sem], inc)
                    if eng == "sync":
                        e.wait_ge(sems["sy"], final_sy)
                return body
            getattr(block, eng)(make())
    es.close()
    return nc


_NC_CACHE = {}


def _get_nc():
    key = (DEBUG,)
    if key not in _NC_CACHE:
        _NC_CACHE[key] = build_program()
    return _NC_CACHE[key]


def _prep(L0_attn_hidden_states, L0_attn_memories, L0_attn_memory_norms,
          L0_attn_w_q, L0_attn_w_k, L0_attn_w_v, L0_attn_w_o, L0_attn_gate,
          L0_ffn_G, L0_ffn_gate_w, L0_ffn_up_w, L0_ffn_down_w,
          L0_ffn_lora_A, L0_ffn_lora_B):
    f32 = np.float32
    asn = lambda a: np.ascontiguousarray(np.asarray(a, dtype=f32))
    x_full = asn(L0_attn_hidden_states).reshape(-1, D)
    shared = {
        "wqT": asn(np.asarray(L0_attn_w_q, f32).T),
        "wkT": asn(np.asarray(L0_attn_w_k, f32).T),
        "wvT": asn(np.asarray(L0_attn_w_v, f32).T),
        "woT": asn(np.asarray(L0_attn_w_o, f32).T),
        "memr": asn(np.transpose(np.asarray(L0_attn_memories, f32), (1, 0, 2, 3))),
        "mnT": asn(np.transpose(np.asarray(L0_attn_memory_norms, f32), (2, 1, 0)).reshape(HD, H * M)),
        "mnF": asn(np.asarray(L0_attn_memory_norms, f32).reshape(M, H * HD)),
        "gatev": asn(np.asarray(L0_attn_gate, f32).reshape(1, H)),
        "G": asn(L0_ffn_G),
        "laT": asn(np.transpose(np.asarray(L0_ffn_lora_A, f32), (2, 0, 1)).reshape(D, 128)),
        "gupTg": asn(np.transpose(np.asarray(L0_ffn_gate_w, f32), (2, 0, 1)).reshape(D, D)),
        "gupTu": asn(np.transpose(np.asarray(L0_ffn_up_w, f32), (2, 0, 1)).reshape(D, D)),
        "downT": asn(np.transpose(np.asarray(L0_ffn_down_w, f32), (0, 2, 1)).reshape(D, D)),
        "lbT": asn(np.transpose(np.asarray(L0_ffn_lora_B, f32), (0, 2, 1)).reshape(128, D)),
    }
    in_maps = []
    for c in range(NCORE):
        m = dict(shared)
        m["x"] = np.ascontiguousarray(x_full[c * T:(c + 1) * T, :])
        m["pmask"] = np.ascontiguousarray(
            (np.arange(NCORE) < c).astype(f32).reshape(1, NCORE))
        in_maps.append(m)
    return in_maps


def kernel(**inputs):
    in_maps = _prep(**inputs)
    nc = _get_nc()
    res = run_bass_kernel_spmd(nc, in_maps, core_ids=list(range(NCORE)))
    out = np.concatenate([np.asarray(r["y"], dtype=np.float32) for r in res.results],
                         axis=0)
    return out.reshape(1, NCORE * T, D)
